# version 13
# baseline (speedup 1.0000x reference)
import math
import numpy as np
import concourse.bass as bass
import concourse.mybir as mybir
from concourse.bass_utils import run_bass_kernel_spmd

F32 = mybir.dt.float32
F16 = mybir.dt.float16
U16 = mybir.dt.uint16
U8 = mybir.dt.uint8
AF = mybir.ActivationFunctionType
ALU = mybir.AluOpType

H = 2048
NK = 128
T = 1024
NT = 8
C = 64
NMLP = 16384
WN = 128
NBLK = NMLP // WN
EPS = 1e-5
SCALE = NK ** -0.5
LNS = math.log(SCALE)



def _bytes_of(packed_i32):
    a = np.ascontiguousarray(packed_i32)
    return np.ascontiguousarray(a.view(np.uint8).reshape(a.shape[0], -1)[:, ::4])


def _npp_order():
    tp = np.arange(64)[:, None, None]
    par = np.arange(2)[None, :, None]
    p = np.arange(128)[None, None, :]
    jp = 128 * tp + p
    j = (jp % 512) * 16 + jp // 512
    n = 2 * j + par
    return n.reshape(-1)


def _host_prep(inputs):
    x = np.asarray(inputs["L0_attn_hidden_states"], np.float32)
    wq = np.asarray(inputs["L0_attn_Wq"], np.float32)
    wk = np.asarray(inputs["L0_attn_Wk"], np.float32)
    wog = np.asarray(inputs["L0_attn_Wog"], np.float32)
    wig = np.asarray(inputs["L0_attn_Wig"], np.float32)
    gwv = np.asarray(inputs["L0_attn_g_weight"], np.float32)

    wq_t = np.ascontiguousarray(wq.T)
    wk_t = np.ascontiguousarray(wk.T)

    def _wblk(w):
        wt = np.ascontiguousarray(w.T)
        return np.ascontiguousarray(wt.reshape(16, 128, 4, 512).transpose(2, 0, 1, 3))

    wog_b = _wblk(wog)
    wig_b = _wblk(wig)
    gw_full = np.ascontiguousarray(np.broadcast_to(gwv[None, :], (128, H)))
    eye = np.eye(128, dtype=np.float32)
    mask = np.triu(np.ones((C, C), dtype=np.uint8))

    npp = _npp_order()

    def _gu_prep(packed, scales):
        b = _bytes_of(packed)
        bt = np.ascontiguousarray(b.T)
        jj = np.arange(1024)
        inv = (jj % 64) * 16 + jj // 64
        bt = bt[inv, :]
        m = np.arange(NMLP) % WN
        base = (np.arange(NMLP) // WN) * WN
        strip = np.where(m % 2 == 0, m // 2, WN // 2 + m // 2)
        cols = npp[base + strip]
        bt = np.ascontiguousarray(bt[:, cols])
        blk = np.ascontiguousarray(bt.reshape(8, 128, NBLK, WN).transpose(2, 1, 0, 3))
        st = np.ascontiguousarray(np.asarray(scales, np.float32).T)
        st = st[:, npp]
        stb = np.ascontiguousarray(st.reshape(64, NBLK, WN).transpose(1, 0, 2))
        return np.ascontiguousarray(blk.reshape(NBLK, 128, 8 * WN)).view('<u2'), stb

    gate_b, gate_s = _gu_prep(inputs["L0_ffn_gate_W_packed"], inputs["L0_ffn_gate_scales"])
    up_b, up_s = _gu_prep(inputs["L0_ffn_up_W_packed"], inputs["L0_ffn_up_scales"])

    db = _bytes_of(np.asarray(inputs["L0_ffn_down_W_packed"]))
    dbt = np.ascontiguousarray(db.T)
    jj = np.arange(8192)
    inv = (jj % 512) * 16 + jj // 512
    dbt = dbt[inv, :]
    down_b = np.ascontiguousarray(dbt.reshape(64, 128, H)).view('<u2')
    hpp = np.r_[0:H:2, 1:H:2]
    ds = np.ascontiguousarray(np.asarray(inputs["L0_ffn_down_scales"], np.float32).T)
    ds = ds[:, hpp]
    down_s = np.ascontiguousarray(ds.reshape(4, 128, H))

    shared = {
        "wq_t": wq_t, "wk_t": wk_t, "wog_b": wog_b, "wig_b": wig_b,
        "gw": gw_full, "eye": eye, "mask": mask,
        "gate_b": gate_b, "gate_s": gate_s, "up_b": up_b, "up_s": up_s,
        "down_b": down_b, "down_s": down_s,
    }
    zeros_half = np.zeros((T, H), np.float32)
    in_maps = []
    for core in range(8):
        b, half = divmod(core, 2)
        m = dict(shared)
        m["x_own"] = np.ascontiguousarray(x[b, T * half:T * (half + 1)])
        m["x_pref"] = np.ascontiguousarray(x[b, 0:T]) if half == 1 else zeros_half
        in_maps.append(m)
    return in_maps



class Ph:
    ENGS = ("sync", "tensor", "vector", "scalar", "gpsimd")

    def __init__(self):
        self.ops = {e: [] for e in self.ENGS}

    def realize(self, block):
        for e in self.ENGS:
            fns = self.ops[e]
            if fns:
                def body(eng, fns=tuple(fns)):
                    for f in fns:
                        f(eng)
                getattr(block, e)(body)


class B:

    def __init__(self, nc):
        self.nc = nc
        self.h = {}
        self.c = {}
        self.ph = None
        for n in ("ld", "sh", "st", "mm", "ve", "ac", "gp",
                  "wc0", "wc1", "wc2", "wc3", "xd0", "xd1", "ldg", "ldd", "st0", "st1"):
            cm = nc.semaphore(n)
            self.h[n] = cm.__enter__()
            self.c[n] = 0

    def em(self, eng, fn, waits=(), inc=None, incby=1, drain=None):
        if drain is None:
            drain = eng in ("vector", "scalar", "gpsimd")
        tgt = None
        if inc is not None:
            self.c[inc] += incby
            tgt = self.c[inc]
        waits = tuple((self.h[n], v) for (n, v) in waits if v is not None)

        def closure(engine, fn=fn, waits=waits, inc=inc, tgt=tgt, incby=incby, drain=drain):
            for (hh, vv) in waits:
                engine.wait_ge(hh, vv)
            ins = fn(engine)
            if drain:
                ins = engine.drain()
            if inc is not None:
                ins.then_inc(self.h[inc], incby)
        self.ph.ops[eng].append(closure)
        return tgt


def build_nc(dbg=(), upto="E", d1_blocks=NBLK):
    nc = bass.Bass("TRN2", target_bir_lowering=False, dynamic_dma_scratch_size=2048)
    bb = B(nc)

    def mm_nl(tensor, out, lhsT, rhs, start, stop):
        return tensor.add_instruction(mybir.InstMatmult(
            name=nc.get_next_instruction_name(),
            ins=[tensor.lower_ap(rhs), tensor.lower_ap(lhsT)],
            outs=[tensor.lower_ap(out)],
            replication_resolution=0, replication_shift_amnt=0,
            replication_num_rows=0,
            start_tensor_calc=start, stop_tensor_calc=stop, ldweights=False))

    hx = nc.dram_tensor("x_own", [T, H], F32, kind="ExternalInput").ap()
    hxp = nc.dram_tensor("x_pref", [T, H], F32, kind="ExternalInput").ap()
    hwq = nc.dram_tensor("wq_t", [H, NK], F32, kind="ExternalInput").ap()
    hwk = nc.dram_tensor("wk_t", [H, NK], F32, kind="ExternalInput").ap()
    hwog = nc.dram_tensor("wog_b", [4, 16, 128, 512], F32, kind="ExternalInput").ap()
    hwig = nc.dram_tensor("wig_b", [4, 16, 128, 512], F32, kind="ExternalInput").ap()
    hgw = nc.dram_tensor("gw", [128, H], F32, kind="ExternalInput").ap()
    heye = nc.dram_tensor("eye", [128, 128], F32, kind="ExternalInput").ap()
    hmask = nc.dram_tensor("mask", [C, C], U8, kind="ExternalInput").ap()
    hgb = nc.dram_tensor("gate_b", [NBLK, 128, 4 * WN], U16, kind="ExternalInput").ap()
    hgs = nc.dram_tensor("gate_s", [NBLK, 64, WN], F32, kind="ExternalInput").ap()
    hub = nc.dram_tensor("up_b", [NBLK, 128, 4 * WN], U16, kind="ExternalInput").ap()
    hus = nc.dram_tensor("up_s", [NBLK, 64, WN], F32, kind="ExternalInput").ap()
    hdb = nc.dram_tensor("down_b", [64, 128, H // 2], U16, kind="ExternalInput").ap()
    hds = nc.dram_tensor("down_s", [4, 128, H], F32, kind="ExternalInput").ap()
    hy = nc.dram_tensor("y", [T, H], F32, kind="ExternalOutput").ap()
    hx2 = nc.dram_tensor("x2_scratch", [T, H], F32).ap()

    D = {}

    def dbg_t(name, shape):
        if name in dbg:
            D[name] = nc.dram_tensor("dbg_" + name, shape, F32, kind="ExternalOutput").ap()

    dbg_t("sgo", [T, H]); dbg_t("attn", [T, H]); dbg_t("rms", [128, NT])
    dbg_t("ln2t", [128, 16 * T])
    dbg_t("o", [T, H]); dbg_t("q", [NK, T]); dbg_t("k", [NK, 2 * T])
    dbg_t("ct", [NK, 2 * T]); dbg_t("S", [NK, H]); dbg_t("x2", [T, H])
    dbg_t("acc", [T, H]); dbg_t("wg", [128, 16 * WN]); dbg_t("wd", [128, 2 * H])
    dbg_t("inter", [256, T])

    big0 = nc.alloc_sbuf_tensor("big0", [128, NT * H], F32)
    big1 = nc.alloc_sbuf_tensor("big1", [128, NT * H], F32)
    x_sb = big0.ap().rearrange("p (a h) -> p a h", a=NT)
    xp_sb = big1.ap().rearrange("p (a h) -> p a h", a=NT)
    o_sb = xp_sb
    ln2t_sb = big0.ap().rearrange("p (a t) -> p a t", a=16)
    ln2t16 = big0.ap().bitcast(F16).rearrange("p (c a t) -> p c a t", c=2, a=16)
    acc_sb = xp_sb

    early_cms = []

    def _sb(name, shape, dt=F32):
        cm = nc.sbuf_tensor(name, shape, dt)
        early_cms.append(cm)
        return cm.__enter__()

    if True:
        eye_sb = _sb("eyet", [128, 128])
        mask_sb = _sb("maskt", [C, C], U8)
        st1 = _sb("stat1", [128, NT, 2])
        st1p = _sb("stat1p", [128, NT, 2])
        st2 = _sb("stat2", [128, NT, 2])
        rms_sb = _sb("rms2", [128, NT])
        bn_sb = _sb("bnb", [128, 16, 27])
        zc_sb = _sb("zcol", [128, 2])
        cst_sb = _sb("cstc", [128, 4])
        ab_cms = []
        def _sba(name, shape, dt=F32):
            cm = nc.sbuf_tensor(name, shape, dt)
            ab_cms.append(cm)
            return cm.__enter__()
        gw_sb = _sba("gwt", [128, H])
        q_sb = _sba("qbuf", [128, T])
        k_sb = _sba("kbuf", [128, 2 * T])
        ct_sb = _sba("ctbuf", [128, 2 * T])
        a_cms = []
        def _sbq(name, shape, dt=F32):
            cm = nc.sbuf_tensor(name, shape, dt)
            a_cms.append(cm)
            return cm.__enter__()
        wq_sb = _sbq("wq_sb", [128, 16, NK])
        wk_sb = _sbq("wk_sb", [128, 16, NK])
        mov_sb = _sbq("mov", [128, 2, 512])
        t_scr = _sbq("tscr", [128, 128])
        sp_sb = _sbq("spbuf", [128, 512])
        sp2_sb = _sbq("spbuf2", [128, 512])
        ones_sb = _sbq("onesb", [128, 512])
        with nc.Block("phAB") as blk:
            bb.ph = Ph()
            em = bb.em
            psA_cm = [nc.psum_tensor("psq", [128, 512], F32),
                      nc.psum_tensor("psk", [128, 512], F32),
                      nc.psum_tensor("pstr", [128, 1024], F32)]
            ps_q, ps_k, ps_t = [cm.__enter__() for cm in psA_cm]

            def dma_in(sync):
                ins = None
                for out, src in (
                    (x_sb[:], hx.rearrange("(a p) h -> p a h", p=128)),
                    (xp_sb[:], hxp.rearrange("(a p) h -> p a h", p=128)),
                    (wq_sb[:], hwq.rearrange("(a p) k -> p a k", p=128)),
                    (wk_sb[:], hwk.rearrange("(a p) k -> p a k", p=128)),
                    (gw_sb[:], hgw[:]),
                    (eye_sb[:], heye[:]),
                    (mask_sb[:], hmask[:]),
                ):
                    ins = sync.dma_start(out=out, in_=src)
                    ins.then_inc(bb.h["ld"], 16)
                return ins
            bb.ph.ops["sync"].append(lambda sync: dma_in(sync))
            bb.c["ld"] += 7 * 16
            n_in = bb.c["ld"]

            def ln1_stats(vector):
                vector.memset(zc_sb[:], 0.0)
                vector.memset(cst_sb[:, 0:1], -1.0)
                vector.memset(cst_sb[:, 1:2], LNS)
                vector.memset(cst_sb[:, 2:3], 1.0 / H)
                vector.memset(cst_sb[:, 3:4], EPS)
                vector.memset(ones_sb[:], 1.0)
                for i in range(16):
                    src = x_sb if i < 8 else xp_sb
                    for s4 in range(4):
                        vector.bn_stats(out=bn_sb[:, i, 6 * s4:6 * s4 + 6],
                                        in_=src[:, i % 8, 512 * s4:512 * (s4 + 1)])
                vector.drain()
                for i in range(16):
                    vector.bn_aggr(out=bn_sb[:, i, 24:26], in_=bn_sb[:, i, 0:24])
                vector.drain()
                for i in range(16):
                    dst = st1 if i < 8 else st1p
                    vector.tensor_copy(out=dst[:, i % 8, 0:1], in_=bn_sb[:, i, 24:25])
                ins = None
                for i in range(16):
                    ins = vector.tensor_scalar(out=bn_sb[:, i, 26:27], in0=bn_sb[:, i, 25:26],
                                               scalar1=1.0, scalar2=EPS, op0=ALU.mult, op1=ALU.add)
                return ins
            v1 = em("vector", ln1_stats, waits=[("ld", n_in)], inc="ve")

            def ln1_sqrt(scalar):
                ins = None
                for i in range(16):
                    ins = scalar.activation(out=bn_sb[:, i, 26:27], in_=bn_sb[:, i, 26:27], func=AF.Sqrt)
                return ins
            a1 = em("scalar", ln1_sqrt, waits=[("ve", v1)], inc="ac")

            def ln1_recip(vector):
                ins = None
                for i in range(16):
                    dst = st1 if i < 8 else st1p
                    ins = vector.reciprocal(out=dst[:, i % 8, 1:2], in_=bn_sb[:, i, 26:27])
                return ins
            v_ln = em("vector", ln1_recip, waits=[("ac", a1)], inc="ve")

            quarter_free = [0, 0, 0, 0]
            slot_free = [0, 0]
            last_tr = 0
            last_evac = [0, 0]
            qk_done = []
            for g in range(4):
                own = g >= 2
                src = x_sb if own else xp_sb
                stat = st1 if own else st1p
                tbase = (g % 2) * 4
                for h in range(16):
                    slot = h % 2
                    v_ev = 0
                    for tt in range(4):
                        qcol = (tt % 2) * 512 + (tt // 2) * 256

                        def ts_slice(vector, j=tbase + tt, h=h, src=src, stat=stat):
                            return vector.tensor_scalar(
                                out=t_scr[:], in0=src[:, j, 128 * h:128 * (h + 1)],
                                scalar1=stat[:, j, 0:1], scalar2=stat[:, j, 1:2],
                                op0=ALU.subtract, op1=ALU.mult)
                        w = [("ve", v_ln if (g, h, tt) == (0, 0, 0) else 0), ("mm", last_tr)]
                        v_ts = em("vector", ts_slice, waits=w, inc="ve")

                        def transp(tensor, qcol=qcol):
                            return tensor.transpose(ps_t[:, qcol:qcol + 128], t_scr[:], eye_sb[:])
                        last_tr = em("tensor", transp,
                                     waits=[("ve", v_ts), ("ve", quarter_free[tt])], inc="mm")

                        def evac(vector, tt=tt, slot=slot, qcol=qcol):
                            return vector.tensor_copy(out=mov_sb[:, slot, 128 * tt:128 * (tt + 1)],
                                                      in_=ps_t[:, qcol:qcol + 128])
                        wv = [("mm", last_tr)]
                        if tt == 0:
                            wv.append(("mm", slot_free[slot]))
                        v_ev = em("vector", evac, waits=wv, inc="ve")
                        quarter_free[tt] = v_ev

                    def mm_qk(tensor, g=g, h=h, slot=slot, own=own):
                        if own:
                            tensor.matmul(ps_q[:], wq_sb[:, h, :], mov_sb[:, slot, :],
                                          start=(h == 0), stop=(h == 15))
                        return tensor.matmul(ps_k[:], wk_sb[:, h, :], mov_sb[:, slot, :],
                                             start=(h == 0), stop=(h == 15))
                    wmm = [("ve", v_ev)]
                    if h == 0:
                        wmm += [("ac", last_evac[0]), ("ac", last_evac[1])]
                    m_qk = em("tensor", mm_qk, waits=wmm, inc="mm")
                    slot_free[slot] = m_qk

                kcols = slice(512 * g, 512 * (g + 1))

                def ev_k(scalar, kcols=kcols):
                    return scalar.activation(out=k_sb[:, kcols], in_=ps_k[:], func=AF.Sigmoid)
                a_k = em("scalar", ev_k, waits=[("mm", m_qk)], inc="ac")
                last_evac[0] = a_k

                if own:
                    def ev_q(scalar, qcols=slice(512 * (g - 2), 512 * (g - 1))):
                        return scalar.activation(out=q_sb[:, qcols], in_=ps_q[:], func=AF.Silu)
                    a_q = em("scalar", ev_q, inc="ac")
                    last_evac[1] = a_q

                def ev_sp(scalar, kcols=kcols):
                    scalar.drain()
                    scalar.activation(out=sp_sb[:], in_=k_sb[:, kcols], func=AF.Sigmoid,
                                      scale=cst_sb[:, 0:1])
                    scalar.drain()
                    return scalar.activation(out=sp2_sb[:], in_=sp_sb[:], func=AF.Ln)
                a_sp = em("scalar", ev_sp, inc="ac")

                def scan(vector, g=g, kcols=kcols):
                    vector.drain()
                    init = 0.0 if g == 0 else ct_sb[:, 512 * g - 1:512 * g]
                    return vector.tensor_tensor_scan(out=ct_sb[:, kcols], data0=ones_sb[:],
                                                     data1=sp2_sb[:], initial=init,
                                                     op0=ALU.mult, op1=ALU.add)
                v_sc = em("vector", scan, waits=[("ac", a_sp)], inc="ve")
            v_ct = v_sc

            bb.ph.realize(blk)
        for cm in reversed(psA_cm):
            cm.__exit__(None, None, None)
        for cm in reversed(a_cms):
            cm.__exit__(None, None, None)
        b_cms = []
        def _sbb(name, shape, dt=F32):
            cm = nc.sbuf_tensor(name, shape, dt)
            b_cms.append(cm)
            return cm.__enter__()
        S_sb = _sbb("sbufS", [128, H])
        ln_scr = _sbb("lnscr", [128, H])
        v_scr = _sbb("vscr", [64, H])
        sp_sb = _sbb("spbufB", [128, 512])
        e1i = _sbb("e1i", [128, C])
        e2i = _sbb("e2i", [128, C])
        expq = _sbb("expq", [128, C])
        expk = _sbb("expk", [128, C])
        exp2 = _sbb("exp2", [128, C])
        dc_sb = _sbb("dcol", [128, 2])
        qt_sb = _sbb("qt", [128, C])
        kt_sb = _sbb("kt", [128, C])
        kh_sb = _sbb("khat", [128, C])
        ktT_sb = _sbb("ktT", [64, 128])
        pt_sb = _sbb("ptsb", [64, C])
        psB_cm = [nc.psum_tensor("psa", [128, H], F32),
                  nc.psum_tensor("pso", [128, H], F32)]
        ps_a, ps_o = [cm.__enter__() for cm in psB_cm]
        with nc.Block("phB") as blk:
            bb.ph = Ph()
            v_S0 = em("vector", lambda v: v.memset(S_sb[:], 0.0), inc="ve")

            t1_prev = t2_prev = t3_prev = t4_prev = 0
            v3_prev = v4_prev = v5_prev = 0
            v6_prev = v_S0
            sh_prev = 0
            v_ln_scr = 0
            s1_prev = 0
            v5_list = []
            for ch in range(32):
                own = ch >= 16
                par = ch % 2
                tile = (ch % 16) // 2
                src = x_sb if own else xp_sb
                stat = st1 if own else st1p
                c0 = C * ch
                oc0 = C * (ch - 16)

                def fn_v1(vector, ch=ch, own=own, par=par, tile=tile, src=src, stat=stat, c0=c0):
                    vector.tensor_scalar(out=e2i[:], in0=ct_sb[:, c0:c0 + C],
                                         scalar1=ct_sb[:, c0 + C - 1:c0 + C], scalar2=None,
                                         op0=ALU.subtract)
                    if own:
                        vector.tensor_scalar(out=e1i[:], in0=ct_sb[:, c0:c0 + C],
                                             scalar1=(ct_sb[:, c0 - 1:c0] if ch > 0 else zc_sb[:, 0:1]),
                                             scalar2=None, op0=ALU.subtract)
                    prev = ct_sb[:, c0 - 1:c0] if ch > 0 else zc_sb[:, 0:1]
                    ins = vector.tensor_scalar(out=dc_sb[:, 0:1], in0=prev,
                                               scalar1=ct_sb[:, c0 + C - 1:c0 + C], scalar2=None,
                                               op0=ALU.subtract)
                    if par == 0:
                        ins = vector.tensor_scalar(out=ln_scr[:], in0=src[:, tile, :],
                                                   scalar1=stat[:, tile, 0:1], scalar2=stat[:, tile, 1:2],
                                                   op0=ALU.subtract, op1=ALU.mult)
                    return ins
                wv1 = [("ac", s1_prev), ("ve", v_ct if ch == 0 else 0)]
                if par == 0 and ch >= 2:
                    wv1 += [("sh", sh_prev), ("mm", t4_prev)]
                v1c = em("vector", fn_v1, waits=wv1, inc="ve")
                if par == 0:
                    v_ln_scr = v1c

                def fn_s1(scalar, own=own):
                    scalar.activation(out=exp2[:], in_=e2i[:], func=AF.Exp, scale=cst_sb[:, 0:1])
                    if own:
                        scalar.activation(out=expq[:], in_=e1i[:], func=AF.Exp, bias=cst_sb[:, 1:2])
                        scalar.activation(out=expk[:], in_=e1i[:], func=AF.Exp, scale=cst_sb[:, 0:1])
                    return scalar.activation(out=dc_sb[:, 1:2], in_=dc_sb[:, 0:1], func=AF.Exp,
                                             scale=cst_sb[:, 0:1])
                s1c = em("scalar", fn_s1, waits=[("ve", v1c)], inc="ac")
                s1_prev = s1c

                def fn_v2(vector, own=own, c0=c0, oc0=oc0):
                    ins = vector.tensor_tensor(out=kh_sb[:], in0=k_sb[:, c0:c0 + C], in1=exp2[:], op=ALU.mult)
                    if own:
                        vector.tensor_tensor(out=qt_sb[:], in0=q_sb[:, oc0:oc0 + C], in1=expq[:], op=ALU.mult)
                        ins = vector.tensor_tensor(out=kt_sb[:], in0=k_sb[:, c0:c0 + C], in1=expk[:], op=ALU.mult)
                    return ins
                v2c = em("vector", fn_v2,
                         waits=[("ac", s1c), ("mm", t1_prev), ("mm", t2_prev)], inc="ve")

                if par == 1:
                    def fn_sh(scalar):
                        return scalar.dma_start(out=v_scr[:], in_=ln_scr[64:128, :])
                    sh_prev = em("scalar", fn_sh, waits=[("ve", v_ln_scr), ("mm", t4_prev)],
                                 inc="sh", incby=16, drain=False)

                def fn_t1(tensor):
                    return tensor.transpose(ps_a[0:64, 0:128], kh_sb[:], eye_sb[:])
                t1c = em("tensor", fn_t1, waits=[("ve", v2c), ("ve", v6_prev)], inc="mm")
                t1_prev = t1c

                v3c = em("vector", lambda v: v.tensor_copy(out=ktT_sb[:], in_=ps_a[0:64, 0:128]),
                         waits=[("mm", t1c)], inc="ve")
                v3_prev = v3c

                if own:
                    def fn_t2(tensor):
                        return tensor.matmul(ps_a[0:64, 0:C], kt_sb[:], qt_sb[:], start=True, stop=True)
                    t2c = em("tensor", fn_t2, waits=[("ve", v3c)], inc="mm")
                    t2_prev = t2c

                    def fn_v4(vector):
                        vector.memset(pt_sb[:], 0.0)
                        vector.drain()
                        return vector.copy_predicated(out=pt_sb[:], mask=mask_sb[:], data=ps_a[0:64, 0:C])
                    v4c = em("vector", fn_v4, waits=[("mm", t2c)], inc="ve")
                    v4_prev = v4c

                    vsel = v_scr if par == 1 else ln_scr
                    def fn_t3(tensor, par=par, vsel=vsel):
                        tp = (0, 64 * par)
                        r = slice(64 * par, 64 * par + 64)
                        ins = None
                        for hc in range(4):
                            cols = slice(512 * hc, 512 * (hc + 1))
                            tensor.matmul(ps_o[r, cols], qt_sb[:], S_sb[:, cols],
                                          start=True, stop=False, tile_position=tp)
                            ins = tensor.matmul(ps_o[r, cols], pt_sb[:], vsel[0:64, cols],
                                                start=False, stop=True, tile_position=tp)
                        return ins
                    wt3 = [("ve", v4c), ("ve", v6_prev), ("ve", v5_prev)]
                    wt3.append(("sh", sh_prev) if par == 1 else ("ve", v_ln_scr))
                    t3c = em("tensor", fn_t3, waits=wt3, inc="mm")
                    t3_prev = t3c

                    def fn_v5(vector, par=par, tile=tile):
                        r = slice(64 * par, 64 * par + 64)
                        return vector.tensor_copy(out=o_sb[r, tile, :], in_=ps_o[r, :])
                    v5c = em("vector", fn_v5, waits=[("mm", t3c)], inc="ve")
                    v5_prev = v5c
                    v5_list.append(v5c)

                vsel = v_scr if par == 1 else ln_scr
                def fn_t4(tensor, vsel=vsel):
                    ins = None
                    for hc in range(4):
                        cols = slice(512 * hc, 512 * (hc + 1))
                        ins = tensor.matmul(ps_a[:, cols], ktT_sb[:], vsel[0:64, cols],
                                            start=True, stop=True)
                    return ins
                wt4 = [("ve", v3c), ("ve", v4_prev if own else 0)]
                wt4.append(("sh", sh_prev) if par == 1 else ("ve", v_ln_scr))
                if own:
                    wt4.append(("mm", t3c))
                t4c = em("tensor", fn_t4, waits=wt4, inc="mm")
                t4_prev = t4c

                def fn_v6(vector):
                    return vector.scalar_tensor_tensor(out=S_sb[:], in0=S_sb[:], scalar=dc_sb[:, 1:2],
                                                       in1=ps_a[:], op0=ALU.mult, op1=ALU.add)
                v6c = em("vector", fn_v6, waits=[("mm", t4c), ("ac", s1c)], inc="ve")
                v6_prev = v6c

            dbg_srcs = {"q": q_sb[:], "k": k_sb[:], "ct": ct_sb[:], "S": S_sb[:]}
            for name in ("q", "k", "ct", "S", "o"):
                if name not in D:
                    continue
                if name == "o":
                    pair = (D["o"].rearrange("(a p) h -> p a h", p=128), o_sb[:])
                else:
                    pair = (D[name], dbg_srcs[name])
                def fn_dbg(sync, pair=pair):
                    return sync.dma_start(out=pair[0], in_=pair[1])
                em("sync", fn_dbg, waits=[("ve", bb.c["ve"]), ("ac", bb.c["ac"])],
                   inc="st", incby=16)
            if upto == "B":
                em("gpsimd", lambda g: g.nop(), waits=[("st", bb.c["st"])])
            bb.ph.realize(blk)
        for cm in reversed(psB_cm):
            cm.__exit__(None, None, None)
        for cm in reversed(b_cms):
            cm.__exit__(None, None, None)
        if upto == "B":
            for cm in reversed(ab_cms):
                cm.__exit__(None, None, None)
            return nc, D

        c_cms = []
        def _sbc(name, shape, dt=F32):
            cm = nc.sbuf_tensor(name, shape, dt)
            c_cms.append(cm)
            return cm.__enter__()
        wqt_sb = _sbc("wquart", [128, 4, 512])
        w16h_c = _sbc("w16h", [128, 4, 512], F16)
        w16l_c = _sbc("w16l", [128, 4, 512], F16)
        t4_scr = _sbc("t4scr", [128, 2, 512])
        sth_c = _sbc("sthc", [128, 2, 4, 128], F16)
        stl_c = _sbc("stlc", [128, 2, 4, 128], F16)
        sgo_sb = _sbc("sgo", [128, 512])
        ep1_sb = _sbc("ep1", [128, 512])
        ep2_sb = _sbc("ep2", [128, 512])
        ysc_sb = _sbc("yscr", [128, 2, 512])
        sq_scr = _sbc("sqscr", [128, H])
        psC_cm = [nc.psum_tensor("psc", [128, H], F32),
                  nc.psum_tensor("psctr", [128, 2, 512], F32)]
        ps_c, ps_ctr = [cm.__enter__() for cm in psC_cm]

        with nc.Block("phC") as blk:
            bb.ph = Ph()

            def c_rms1(vector):
                ins = None
                for t in range(NT):
                    vector.tensor_tensor(out=sq_scr[:], in0=o_sb[:, t, :], in1=o_sb[:, t, :], op=ALU.mult)
                    vector.drain()
                    ins = vector.tensor_reduce(out=bn_sb[:, t, 0:1], in_=sq_scr[:], op=ALU.add,
                                               axis=mybir.AxisListType.X)
                    vector.drain()
                return ins
            vr1 = em("vector", c_rms1, inc="ve")

            def c_rms2(scalar):
                ins = None
                for t in range(NT):
                    ins = scalar.activation(out=bn_sb[:, t, 1:2], in_=bn_sb[:, t, 0:1], func=AF.Sqrt,
                                            scale=cst_sb[:, 2:3], bias=cst_sb[:, 3:4])
                return ins
            ar2 = em("scalar", c_rms2, waits=[("ve", vr1)], inc="ac")

            def c_rms3(vector):
                ins = None
                for t in range(NT):
                    ins = vector.reciprocal(out=rms_sb[:, t:t + 1], in_=bn_sb[:, t, 1:2])
                return ins
            vr3 = em("vector", c_rms3, waits=[("ac", ar2)], inc="ve")
            if "rms" in D:
                em("sync", lambda s: s.dma_start(out=D["rms"], in_=rms_sb[:]),
                   waits=[("ve", vr3)], inc="st", incby=16)

            wslot_free = [0, 0, 0, 0]
            quarter_free = [0, 0, 0, 0]
            grp_used = [0, 0]
            tr_done = [0, 0]
            evh_done = [0, 0]
            evl_done = [0, 0]
            y_free = [0, 0]
            sgo_used = 0
            sgo_sem = "st" if "sgo" in D else "ve"
            for q4 in range(4):
                qcols = slice(512 * q4, 512 * (q4 + 1))
                for tg in range(2):
                    mm_c_last = 0
                    pend_mm = None
                    for mat in range(2):
                        hw = hwog if mat == 0 else hwig
                        for kk in range(16):
                            slot = kk % 4
                            grp = kk % 2
                            wsem = "wc%d" % slot

                            def fn_wld(sync, hw=hw, q4=q4, kk=kk, slot=slot):
                                return sync.dma_start(out=wqt_sb[:, slot, :], in_=hw[q4, kk])
                            ld_c = em("sync", fn_wld, waits=[("mm", wslot_free[slot])],
                                      inc=wsem, incby=16)

                            def fn_wh(scalar, slot=slot):
                                return scalar.copy(out=w16h_c[:, slot, :], in_=wqt_sb[:, slot, :])
                            a_wh = em("scalar", fn_wh,
                                      waits=[(wsem, ld_c), ("mm", wslot_free[slot])], inc="ac")

                            def fn_wl(vector, slot=slot):
                                return vector.tensor_tensor(out=w16l_c[:, slot, :],
                                                            in0=wqt_sb[:, slot, :],
                                                            in1=w16h_c[:, slot, :], op=ALU.subtract)
                            v_wl = em("vector", fn_wl, waits=[("ac", a_wh)], inc="ve")

                            if mat == 1:
                                def fn_ts4(vector, tg=tg, kk=kk, grp=grp):
                                    ins = None
                                    for ti in range(4):
                                        ins = vector.tensor_scalar(
                                            out=t4_scr[:, grp, 128 * ti:128 * (ti + 1)],
                                            in0=x_sb[:, tg * 4 + ti, 128 * kk:128 * (kk + 1)],
                                            scalar1=st1[:, tg * 4 + ti, 0:1],
                                            scalar2=st1[:, tg * 4 + ti, 1:2],
                                            op0=ALU.subtract, op1=ALU.mult)
                                    return ins
                                v_ts = em("vector", fn_ts4, waits=[("mm", tr_done[grp])], inc="ve")

                            def fn_tr4(tensor, mat=mat, tg=tg, kk=kk, grp=grp):
                                ins = None
                                for ti in range(4):
                                    src_ap = (t4_scr[:, grp, 128 * ti:128 * (ti + 1)] if mat == 1
                                              else o_sb[:, tg * 4 + ti, 128 * kk:128 * (kk + 1)])
                                    ins = tensor.transpose(ps_ctr[:, grp, 128 * ti:128 * (ti + 1)],
                                                           src_ap, eye_sb[:])
                                return ins
                            trw = [("ac", evh_done[grp]), ("ve", evl_done[grp])]
                            if mat == 1:
                                trw.append(("ve", v_ts))
                            tr_done[grp] = em("tensor", fn_tr4, waits=trw, inc="mm")

                            def fn_eh4(scalar, grp=grp):
                                ins = None
                                for ti in range(4):
                                    ins = scalar.copy(out=sth_c[:, grp, ti, :],
                                                      in_=ps_ctr[:, grp, 128 * ti:128 * (ti + 1)])
                                return ins
                            evh_done[grp] = em("scalar", fn_eh4,
                                               waits=[("mm", tr_done[grp]), ("mm", grp_used[grp])],
                                               inc="ac")

                            def fn_el4(vector, grp=grp):
                                ins = None
                                for ti in range(4):
                                    ins = vector.tensor_tensor(out=stl_c[:, grp, ti, :],
                                                               in0=ps_ctr[:, grp, 128 * ti:128 * (ti + 1)],
                                                               in1=sth_c[:, grp, ti, :],
                                                               op=ALU.subtract)
                                return ins
                            evl_done[grp] = em("vector", fn_el4, waits=[("ac", evh_done[grp])],
                                               inc="ve")

                            def fn_mm4(tensor, grp=grp, slot=slot, mat=mat, kk=kk, tg=tg):
                                ins = None
                                for ti in range(4):
                                    tq = (tg * 4 + ti) % 4
                                    tensor.ldweights(sth_c[:, grp, ti, :])
                                    ins = mm_nl(tensor, ps_c[:, 512 * tq:512 * tq + 512],
                                                sth_c[:, grp, ti, :], w16h_c[:, slot, :],
                                                start=(mat == 0 and kk == 0), stop=False)
                                    ins = mm_nl(tensor, ps_c[:, 512 * tq:512 * tq + 512],
                                                sth_c[:, grp, ti, :], w16l_c[:, slot, :],
                                                start=False, stop=False)
                                    tensor.ldweights(stl_c[:, grp, ti, :])
                                    ins = mm_nl(tensor, ps_c[:, 512 * tq:512 * tq + 512],
                                                stl_c[:, grp, ti, :], w16h_c[:, slot, :],
                                                start=False, stop=(mat == 1 and kk == 15))
                                return ins
                            wmm = [("ve", evl_done[grp]), ("ve", v_wl)]
                            if mat == 0 and kk == 0:
                                wmm += [("ac", quarter_free[i]) for i in range(4)]
                            if pend_mm is not None:
                                pfn, pw, pslot, pgrp = pend_mm
                                mm_c_last = em("tensor", pfn, waits=pw, inc="mm")
                                grp_used[pgrp] = mm_c_last
                                wslot_free[pslot] = mm_c_last
                            pend_mm = (fn_mm4, wmm, slot, grp)
                    pfn, pw, pslot, pgrp = pend_mm
                    mm_c_last = em("tensor", pfn, waits=pw, inc="mm")
                    grp_used[pgrp] = mm_c_last
                    wslot_free[pslot] = mm_c_last

                    for ti in range(4):
                        t = tg * 4 + ti

                        def fn_si(scalar, t=t):
                            return scalar.activation(out=sgo_sb[:], in_=ps_c[:, 512 * (t % 4):512 * (t % 4) + 512],
                                                     func=AF.Silu)
                        a_si = em("scalar", fn_si, waits=[("mm", mm_c_last), (sgo_sem, sgo_used)], inc="ac")
                        quarter_free[t % 4] = a_si

                        def fn_e1(vector, t=t, qcols=qcols):
                            return vector.scalar_tensor_tensor(out=ep1_sb[:], in0=o_sb[:, t, qcols],
                                                               scalar=rms_sb[:, t:t + 1], in1=gw_sb[:, qcols],
                                                               op0=ALU.mult, op1=ALU.mult)
                        v_e1 = em("vector", fn_e1, waits=[("ve", vr3 if (q4, tg, ti) == (0, 0, 0) else 0)], inc="ve")

                        def fn_e2(vector):
                            return vector.tensor_tensor(out=ep2_sb[:], in0=ep1_sb[:], in1=sgo_sb[:], op=ALU.mult)
                        v_e2 = em("vector", fn_e2, waits=[("ac", a_si)], inc="ve")
                        sgo_used = v_e2
                        if "sgo" in D:
                            def fn_dsg(sync, t=t, q4=q4):
                                return sync.dma_start(
                                    out=D["sgo"].rearrange("(a p) h -> p a h", p=128)[:, t, 512 * q4:512 * (q4 + 1)],
                                    in_=sgo_sb[:])
                            sgo_used = em("sync", fn_dsg, waits=[("ve", v_e2)], inc="st", incby=16)
                        if "attn" in D:
                            def fn_dat(sync, t=t, q4=q4):
                                return sync.dma_start(
                                    out=D["attn"].rearrange("(a p) h -> p a h", p=128)[:, t, 512 * q4:512 * (q4 + 1)],
                                    in_=ep2_sb[:])
                            em("sync", fn_dat, waits=[("ve", v_e2)], inc="st", incby=16)

                        yslot = t % 2

                        def fn_e3(vector, t=t, qcols=qcols, yslot=yslot):
                            return vector.tensor_tensor(out=ysc_sb[:, yslot, :], in0=x_sb[:, t, qcols],
                                                        in1=ep2_sb[:], op=ALU.add)
                        v_e3 = em("vector", fn_e3, waits=[("st%d" % yslot, y_free[yslot])], inc="ve")

                        def fn_x2st(sync, t=t, q4=q4, yslot=yslot):
                            return sync.dma_start(
                                out=hx2.rearrange("(a p) h -> p a h", p=128)[:, t, 512 * q4:512 * (q4 + 1)],
                                in_=ysc_sb[:, yslot, :])
                        y_free[yslot] = em("sync", fn_x2st, waits=[("ve", v_e3)],
                                           inc="st%d" % yslot, incby=16)
            st_after_c = (bb.c["st0"], bb.c["st1"])
            if upto == "C":
                def fn_dx2(sync):
                    return sync.dma_start(out=D["x2"], in_=hx2[:])
                if "x2" in D:
                    em("sync", fn_dx2, waits=[("st", st_after_c)], inc="st", incby=16)
                em("gpsimd", lambda g: g.nop(), waits=[("st", bb.c["st"])])
            bb.ph.realize(blk)
        for cm in reversed(psC_cm):
            cm.__exit__(None, None, None)
        for cm in reversed(c_cms):
            cm.__exit__(None, None, None)
        for cm in reversed(ab_cms):
            cm.__exit__(None, None, None)
        if upto == "C":
            return nc, D

        d0_cms = []
        def _sd0(name, shape, dt=F32):
            cm = nc.sbuf_tensor(name, shape, dt)
            d0_cms.append(cm)
            return cm.__enter__()
        x2r_sb = _sd0("x2r", [128, 2, H])
        ln2s_sb = _sd0("ln2s", [128, H])
        g_scr = _sd0("gscr", [128, 128])
        psD0_cm = [nc.psum_tensor("psd0", [128, 512], F32)]
        ps_d0 = psD0_cm[0].__enter__()

        with nc.Block("phD0") as blk:
            bb.ph = Ph()
            ring_free = [0, 0]
            tr_last_d0 = 0
            ev_d0 = 0
            ev_hi_d0 = 0
            ln2_used = 0
            for t in range(NT):
                def fn_ldx2(sync, t=t):
                    return sync.dma_start(out=x2r_sb[:, t % 2, :],
                                          in_=hx2.rearrange("(a p) h -> p a h", p=128)[:, t, :])
                ld_x2 = em("sync", fn_ldx2,
                           waits=[("st0", st_after_c[0]), ("st1", st_after_c[1]),
                                  ("ve", ring_free[t % 2])],
                           inc="xd%d" % (t % 2), incby=16)

                def fn_bn(vector, t=t):
                    for s4 in range(4):
                        vector.bn_stats(out=bn_sb[:, t, 6 * s4:6 * s4 + 6],
                                        in_=x2r_sb[:, t % 2, 512 * s4:512 * (s4 + 1)])
                    vector.drain()
                    vector.bn_aggr(out=bn_sb[:, t, 24:26], in_=bn_sb[:, t, 0:24])
                    vector.drain()
                    vector.tensor_copy(out=st2[:, t, 0:1], in_=bn_sb[:, t, 24:25])
                    return vector.tensor_scalar(out=bn_sb[:, t, 26:27], in0=bn_sb[:, t, 25:26],
                                                scalar1=1.0, scalar2=EPS, op0=ALU.mult, op1=ALU.add)
                v_bn = em("vector", fn_bn, waits=[("xd%d" % (t % 2), ld_x2)], inc="ve")

                def fn_sq(scalar, t=t):
                    return scalar.activation(out=bn_sb[:, t, 26:27], in_=bn_sb[:, t, 26:27], func=AF.Sqrt)
                a_sq = em("scalar", fn_sq, waits=[("ve", v_bn)], inc="ac")

                def fn_rc(vector, t=t):
                    vector.reciprocal(out=st2[:, t, 1:2], in_=bn_sb[:, t, 26:27])
                    vector.drain()
                    return vector.tensor_scalar(out=ln2s_sb[:], in0=x2r_sb[:, t % 2, :],
                                                scalar1=st2[:, t, 0:1], scalar2=st2[:, t, 1:2],
                                                op0=ALU.subtract, op1=ALU.mult)
                v_ln2 = em("vector", fn_rc, waits=[("ac", a_sq)], inc="ve")
                ring_free[t % 2] = v_ln2

                for u in range(8):
                    for par in range(2):
                        def fn_gth(vector, u=u, par=par):
                            sel = bass.AP(ln2s_sb, 4 * u + par, [[2048, 128], [2, 2], [32, 64]])
                            return vector.tensor_copy(out=g_scr[:], in_=sel)
                        v_g = em("vector", fn_gth, waits=[("ve", v_ln2), ("mm", tr_last_d0)], inc="ve")
                        ln2_used = max(ln2_used, v_g)

                        def fn_trp(tensor):
                            return tensor.transpose(ps_d0[:, 0:128], g_scr[:], eye_sb[:])
                        tr_last_d0 = em("tensor", fn_trp,
                                        waits=[("ve", v_g), ("ve", ev_d0), ("ac", ev_hi_d0)], inc="mm")

                        def fn_evh(scalar, u=u, par=par, t=t):
                            return scalar.copy(out=ln2t16[:, 0, 2 * u + par, 128 * t:128 * (t + 1)],
                                               in_=ps_d0[:, 0:128])
                        ev_hi_d0 = em("scalar", fn_evh, waits=[("mm", tr_last_d0)], inc="ac")

                        def fn_evp(vector, u=u, par=par, t=t):
                            return vector.tensor_tensor(
                                out=ln2t16[:, 1, 2 * u + par, 128 * t:128 * (t + 1)],
                                in0=ps_d0[:, 0:128],
                                in1=ln2t16[:, 0, 2 * u + par, 128 * t:128 * (t + 1)],
                                op=ALU.subtract)
                        ev_d0 = em("vector", fn_evp, waits=[("mm", tr_last_d0), ("ac", ev_hi_d0)], inc="ve")
            if "ln2t" in D:
                em("sync", lambda s: s.dma_start(out=D["ln2t"],
                                                 in_=big0.ap().rearrange("p (a t) -> p a t", a=16)[:].rearrange("p a t -> p (a t)")),
                   waits=[("ve", bb.c["ve"]), ("ac", bb.c["ac"])], inc="st", incby=16)
            if upto == "D0":
                em("gpsimd", lambda g: g.nop(), waits=[("st", bb.c["st"])])
            bb.ph.realize(blk)
        psD0_cm[0].__exit__(None, None, None)
        for cm in reversed(d0_cms):
            cm.__exit__(None, None, None)
        for cm in reversed(early_cms):
            cm.__exit__(None, None, None)
        if upto == "D0":
            return nc, D

        d1_cms = []
        def _sd1(name, shape, dt=F32):
            cm = nc.sbuf_tensor(name, shape, dt)
            d1_cms.append(cm)
            return cm.__enter__()
        gWh_sb = _sd1("gateWh", [128, 2, 16, WN], F16)
        gWl_sb = _sd1("gateWl", [128, 2, 16, WN], F16)
        uWh_sb = _sd1("upWh", [128, 2, 16, WN], F16)
        uWl_sb = _sd1("upWl", [128, 2, 16, WN], F16)
        w32_sb = _sd1("w32", [128, 16, WN])
        gby_sb = _sd1("gby", [128, 4 * WN], U16)
        uby_sb = gby_sb
        gsc_sb = _sd1("gsc", [128, WN])
        usc_sb = gsc_sb
        bit4_sb = _sd1("bitp", [128, 2048], U16)
        t1_sb = _sd1("dect1", [128, 2, 1024], U16)
        t2_sb = _sd1("dect2", [128, 2, 1024], U16)
        dby_sb = _sd1("dby", [128, H // 2], U16)
        dWh_sb = _sd1("dWh", [128, 2, H], F16)
        dWl_sb = _sd1("dWl", [128, 2, H], F16)
        dsc_sb = _sd1("dsc", [128, H])
        inth_sb = _sd1("interh", [128, 2, T], F16)
        intl_sb = _sd1("interl", [128, 2, T], F16)
        gel_sb = _sd1("gel", [128, 2, 512])
        i32_sb = gel_sb
        psD1_cm = [nc.psum_tensor("psg", [128, 2, 512], F32),
                   nc.psum_tensor("psu", [128, 2, 512], F32),
                   nc.psum_tensor("psd", [128, H], F32)]
        ps_g, ps_u, ps_d = [cm.__enter__() for cm in psD1_cm]

        DEC_PLANES = ((0x0007, 6, 0, 0x0008, 12), (0x0070, 2, 0, 0x0080, 8),
                      (0x0700, 2, 1, 0x0800, 4), (0x7000, 6, 1, 0x8000, 0))

        def emit_nib_group(vector, by, n, grp):
            for si, s in grp:
                m1, sh1, dirn, m2, sh2 = DEC_PLANES[s]
                shop = ALU.logical_shift_left if dirn == 0 else ALU.logical_shift_right
                vector.tensor_scalar(out=t1_sb[:, si % 2, 0:n], in0=by[:, 0:n], scalar1=m1,
                                     scalar2=sh1, op0=ALU.bitwise_and, op1=shop)
                if sh2:
                    vector.tensor_scalar(out=t2_sb[:, si % 2, 0:n], in0=by[:, 0:n], scalar1=m2,
                                         scalar2=sh2, op0=ALU.bitwise_and, op1=ALU.logical_shift_left)
                else:
                    vector.tensor_scalar(out=t2_sb[:, si % 2, 0:n], in0=by[:, 0:n], scalar1=m2,
                                         scalar2=None, op0=ALU.bitwise_and)
            vector.drain()
            for si, s in grp:
                vector.tensor_scalar(out=bit4_sb[:, si * n:(si + 1) * n], in0=t1_sb[:, si % 2, 0:n],
                                     scalar1=0x40, scalar2=0xFA, op0=ALU.min, op1=ALU.mult)
                vector.tensor_tensor(out=t2_sb[:, si % 2, 0:n], in0=t1_sb[:, si % 2, 0:n],
                                     in1=t2_sb[:, si % 2, 0:n], op=ALU.add)
            vector.drain()
            for si, s in grp:
                vector.scalar_tensor_tensor(out=t1_sb[:, si % 2, 0:n], in0=t1_sb[:, si % 2, 0:n],
                                            scalar=0x80, in1=bit4_sb[:, si * n:(si + 1) * n],
                                            op0=ALU.min, op1=ALU.add)
            vector.drain()
            ins = None
            for si, s in grp:
                ins = vector.tensor_tensor(out=bit4_sb[:, si * n:(si + 1) * n], in0=t1_sb[:, si % 2, 0:n],
                                           in1=t2_sb[:, si % 2, 0:n], op=ALU.add)
            return ins

        def emit_gu_scales(vector, sc):
            n = 4 * WN
            ins = None
            for s in range(4):
                par, half = s % 2, s // 2
                wout = bass.AP(w32_sb, par * WN + half * (WN // 2),
                               [[16 * WN, 128], [2 * WN, 8], [1, WN // 2]])
                scin = bass.AP(sc, half * (WN // 2), [[WN, 128], [0, 8], [1, WN // 2]])
                ins = vector.tensor_tensor(out=wout, in0=bit4_sb[:, s * n:(s + 1) * n].bitcast(mybir.dt.bfloat16),
                                           in1=scin, op=ALU.mult)
            return ins

        def emit_down_scales(vector):
            n = H // 2
            ins = None
            for half in range(2):
                ins = vector.tensor_tensor(out=bass.AP(w32_sb, half * n, [[16 * WN, 128], [1, n]]),
                                           in0=bit4_sb[:, half * n:(half + 1) * n].bitcast(mybir.dt.bfloat16),
                                           in1=dsc_sb[:, half * n:(half + 1) * n], op=ALU.mult)
            return ins

        w32_flat = bass.AP(w32_sb, 0, [[16 * WN, 128], [1, 16 * WN]])

        with nc.Block("phD1") as blk:
            bb.ph = Ph()
            v_acc0 = em("vector", lambda v: v.memset(acc_sb[:], 0.0), inc="ve")
            gw_used = [0, 0]
            uw_used = [0, 0]
            w32_free = 0
            gby_used = 0
            uby_used = 0
            dby_used = 0
            dsc_used = 0
            psg_free = [0, 0]
            psu_free = [0, 0]
            int_hi_done = 0
            int_free = [0, 0]
            gel_free = [0, 0]
            down_done = 0
            ps_d_free = v_acc0
            ST = {"gw_used": [0, 0], "uw_used": [0, 0], "w32_free": 0, "uby_used": 0,
                  "dsc_used": 0, "down_done": 0,
                  "ld_g": 0, "ld_u": 0, "ld_d": 0, "v_dg": 0, "v_dg_by": 0,
                  "a_gh": 0, "a_uh": 0, "a_dh": 0, "v_gl": {}, "v_ul": {}, "v_dd": {}}

            def frags_for(Bk):
                wp = Bk % 2
                fr = []

                def f_ldg(Bk=Bk):
                    def fn(sync):
                        sync.dma_start(out=gby_sb[:], in_=hgb[Bk]).then_inc(bb.h["ldg"], 16)
                        sync.dma_start(out=gsc_sb[0:64, :], in_=hgs[Bk]).then_inc(bb.h["ldg"], 16)
                        return sync.dma_start(out=gsc_sb[64:128, :], in_=hgs[Bk])
                    bb.c["ldg"] += 32
                    ST["ld_g"] = em("sync", fn, waits=[("ve", ST["uby_used"])], inc="ldg", incby=16)
                fr.append(f_ldg)

                def f_dg_a():
                    em("vector", lambda v: emit_nib_group(v, gby_sb, 4 * WN, ((0, 0), (1, 1))),
                       waits=[("ldg", ST["ld_g"])], inc="ve")
                fr.append(f_dg_a)

                def f_dg_b():
                    ST["v_dg_by"] = em("vector",
                                       lambda v: emit_nib_group(v, gby_sb, 4 * WN, ((2, 2), (3, 3))),
                                       inc="ve")
                fr.append(f_dg_b)

                def f_dg_s():
                    ST["v_dg"] = em("vector", lambda v: emit_gu_scales(v, gsc_sb),
                                    waits=[("gp", ST["w32_free"])], inc="ve")
                fr.append(f_dg_s)

                def f_gsplit(Bk=Bk, wp=wp):
                    ST["a_gh"] = em("scalar",
                                    lambda s, wp=wp: s.copy(out=gWh_sb[:, wp], in_=w32_sb[:]),
                                    waits=[("ve", ST["v_dg"]), ("mm", ST["gw_used"][wp])], inc="ac")
                    ST["v_gl"][Bk] = em("gpsimd",
                                        lambda g, wp=wp: g.tensor_tensor(
                                            out=bass.AP(gWl_sb, wp * 16 * WN, [[32 * WN, 128], [1, 16 * WN]]),
                                            in0=w32_flat, in1=bass.AP(gWh_sb, wp * 16 * WN, [[32 * WN, 128], [1, 16 * WN]]),
                                            op=ALU.subtract),
                                        waits=[("ac", ST["a_gh"])], inc="gp")
                    ST["w32_free"] = ST["v_gl"][Bk]
                fr.append(f_gsplit)

                def f_ldu(Bk=Bk):
                    def fn(sync):
                        sync.dma_start(out=uby_sb[:], in_=hub[Bk]).then_inc(bb.h["ldg"], 16)
                        sync.dma_start(out=usc_sb[0:64, :], in_=hus[Bk]).then_inc(bb.h["ldg"], 16)
                        return sync.dma_start(out=usc_sb[64:128, :], in_=hus[Bk])
                    bb.c["ldg"] += 32
                    ST["ld_u"] = em("sync", fn, waits=[("ve", ST["v_dg"])], inc="ldg", incby=16)
                fr.append(f_ldu)

                def f_du_a():
                    em("vector", lambda v: emit_nib_group(v, uby_sb, 4 * WN, ((0, 0), (1, 1))),
                       waits=[("ldg", ST["ld_u"])], inc="ve")
                fr.append(f_du_a)

                def f_du_b():
                    em("vector", lambda v: emit_nib_group(v, uby_sb, 4 * WN, ((2, 2), (3, 3))),
                       inc="ve")
                fr.append(f_du_b)

                def f_du_s():
                    ST["v_du"] = em("vector", lambda v: emit_gu_scales(v, usc_sb),
                                    waits=[("gp", ST["w32_free"])], inc="ve")
                    ST["uby_used"] = ST["v_du"]
                fr.append(f_du_s)

                def f_usplit(Bk=Bk, wp=wp):
                    ST["a_uh"] = em("scalar", lambda s, wp=wp: s.copy(out=uWh_sb[:, wp], in_=w32_sb[:]),
                                    waits=[("ve", ST["v_du"]), ("mm", ST["uw_used"][wp])], inc="ac")
                    ST["v_ul"][Bk] = em("gpsimd",
                                        lambda g, wp=wp: g.tensor_tensor(
                                            out=bass.AP(uWl_sb, wp * 16 * WN, [[32 * WN, 128], [1, 16 * WN]]),
                                            in0=w32_flat, in1=bass.AP(uWh_sb, wp * 16 * WN, [[32 * WN, 128], [1, 16 * WN]]),
                                            op=ALU.subtract),
                                        waits=[("ac", ST["a_uh"])], inc="gp")
                    ST["w32_free"] = ST["v_ul"][Bk]
                fr.append(f_usplit)

                if Bk % 2 == 1:
                    def f_ldd(Bk=Bk):
                        def fn(sync):
                            sync.dma_start(out=dby_sb[:], in_=hdb[Bk // 2]).then_inc(bb.h["ldd"], 16)
                            return sync.dma_start(out=dsc_sb[:], in_=hds[(Bk // 2) % 4])
                        bb.c["ldd"] += 16
                        ST["ld_d"] = em("sync", fn,
                                        waits=[("ve", ST["dsc_used"])],
                                        inc="ldd", incby=16)
                    fr.append(f_ldd)

                    def f_dd_a(cl):
                        em("vector", lambda v, cl=cl: emit_nib_group(v, dby_sb, H // 2,
                                                                     ((0, cl), (1, cl + 2))),
                           waits=[("ldd", ST["ld_d"])], inc="ve")
                    fr.append(lambda: f_dd_a(0))

                    def f_dd_s(cl):
                        ST["v_dds"] = em("vector", emit_down_scales,
                                         waits=[("gp", ST["w32_free"])], inc="ve")
                        if cl == 1:
                            ST["dsc_used"] = ST["v_dds"]
                    fr.append(lambda: f_dd_s(0))

                    def f_dsplit(cl, Bk=Bk):
                        ST["a_dh"] = em("scalar", lambda s, cl=cl: s.copy(
                            out=dWh_sb[:, cl, :], in_=bass.AP(w32_sb, 0, [[16 * WN, 128], [1, H]])),
                            waits=[("ve", ST["v_dds"]), ("mm", ST["down_done"])], inc="ac")
                        ST["v_dd"][Bk * 2 + cl] = em("gpsimd", lambda g, cl=cl: g.tensor_tensor(
                            out=bass.AP(dWl_sb, cl * H, [[2 * H, 128], [1, H]]),
                            in0=bass.AP(w32_sb, 0, [[16 * WN, 128], [1, H]]),
                            in1=bass.AP(dWh_sb, cl * H, [[2 * H, 128], [1, H]]),
                            op=ALU.subtract),
                            waits=[("ac", ST["a_dh"])], inc="gp")
                        ST["w32_free"] = ST["v_dd"][Bk * 2 + cl]
                    fr.append(lambda: f_dsplit(0))
                    fr.append(lambda: f_dd_a(1))
                    fr.append(lambda: f_dd_s(1))
                    fr.append(lambda: f_dsplit(1))
                return fr

            pend = []

            def pump(n):
                for _ in range(n):
                    if pend:
                        pend.pop(0)()

            pend += frags_for(0)
            pump(99)
            int_ready = 0
            for Bk in range(d1_blocks):
                wp = Bk % 2
                islot = Bk % 2
                if Bk + 1 < d1_blocks:
                    pend += frags_for(Bk + 1)

                int_th = [0, 0]
                mgu = [0, 0]
                for th in range(2):
                    def fn_mgu(tensor, wp=wp, th=th):
                        ins = None
                        for Wh, Wl, ps, upi in ((gWh_sb, gWl_sb, ps_g, wp), (uWh_sb, uWl_sb, ps_u, wp)):
                            for kt in range(16):
                                tensor.ldweights(Wh[:, upi, kt, :])
                                for xp in (0, 1):
                                    ins = mm_nl(tensor, ps[:, th, :], Wh[:, upi, kt, :],
                                                ln2t16[:, xp, kt, 512 * th:512 * (th + 1)],
                                                start=(kt == 0 and xp == 0), stop=False)
                                tensor.ldweights(Wl[:, upi, kt, :])
                                ins = mm_nl(tensor, ps[:, th, :], Wl[:, upi, kt, :],
                                            ln2t16[:, 0, kt, 512 * th:512 * (th + 1)],
                                            start=False, stop=(kt == 15))
                        return ins
                    mgu[th] = em("tensor", fn_mgu,
                                 waits=[("gp", ST["v_gl"][Bk]), ("gp", ST["v_ul"][Bk]),
                                        ("ac", psg_free[th]), ("ve", psu_free[th])], inc="mm")
                ST["gw_used"][wp] = mgu[1]
                ST["uw_used"][wp] = mgu[1]

                for th in range(2):
                    def fn_gel(scalar, th=th):
                        return scalar.activation(out=gel_sb[:, th, :], in_=ps_g[:, th, :],
                                                 func=AF.Gelu_apprx_tanh)
                    a_gel = em("scalar", fn_gel, waits=[("mm", mgu[th]), ("gp", gel_free[th])], inc="ac")
                    psg_free[th] = a_gel

                    def fn_i32(vector, th=th):
                        return vector.tensor_tensor(out=i32_sb[:, th, :], in0=gel_sb[:, th, :],
                                                    in1=ps_u[:, th, :], op=ALU.mult)
                    v_i32 = em("vector", fn_i32,
                               waits=[("ac", a_gel), ("mm", mgu[th]), ("ac", int_hi_done)], inc="ve")
                    psu_free[th] = v_i32

                    def fn_ih(scalar, th=th, islot=islot):
                        return scalar.copy(out=inth_sb[:, islot, 512 * th:512 * (th + 1)],
                                           in_=i32_sb[:, th, :])
                    int_hi_done = em("scalar", fn_ih,
                                     waits=[("ve", v_i32), ("mm", int_free[islot])], inc="ac")

                    def fn_il(gp, th=th, islot=islot):
                        return gp.tensor_tensor(out=intl_sb[:, islot, 512 * th:512 * (th + 1)],
                                                in0=i32_sb[:, th, :],
                                                in1=inth_sb[:, islot, 512 * th:512 * (th + 1)],
                                                op=ALU.subtract)
                    int_ready = em("gpsimd", fn_il, waits=[("ac", int_hi_done)], inc="gp")
                    gel_free[th] = int_ready
                    int_th[th] = int_ready
                ST["int_prev"] = ST.get("int_cur", [0, 0])
                ST["int_cur"] = list(int_th)
                if Bk % 2 == 0:
                    pump(99)
                    continue

                for t in range(NT):
                    def fn_md0(tensor, t=t):
                        ins = None
                        cl = 0
                        tensor.ldweights(inth_sb[:, cl, 128 * t:128 * (t + 1)])
                        for wop in (dWh_sb, dWl_sb):
                            for hc in range(4):
                                ins = mm_nl(tensor, ps_d[:, 512 * hc:512 * (hc + 1)],
                                            inth_sb[:, cl, 128 * t:128 * (t + 1)],
                                            wop[:, cl, 512 * hc:512 * (hc + 1)],
                                            start=(wop is dWh_sb), stop=False)
                        tensor.ldweights(intl_sb[:, cl, 128 * t:128 * (t + 1)])
                        for hc in range(4):
                            ins = mm_nl(tensor, ps_d[:, 512 * hc:512 * (hc + 1)],
                                        intl_sb[:, cl, 128 * t:128 * (t + 1)],
                                        dWh_sb[:, cl, 512 * hc:512 * (hc + 1)],
                                        start=False, stop=False)
                        return ins
                    md0 = em("tensor", fn_md0,
                             waits=[("gp", ST["v_dd"][Bk * 2 + 0]),
                                    ("gp", ST["int_prev"][t // 4]),
                                    ("ve", ps_d_free)], inc="mm")

                    def fn_md1(tensor, t=t):
                        ins = None
                        cl = 1
                        tensor.ldweights(inth_sb[:, cl, 128 * t:128 * (t + 1)])
                        for wop in (dWh_sb, dWl_sb):
                            for hc in range(4):
                                ins = mm_nl(tensor, ps_d[:, 512 * hc:512 * (hc + 1)],
                                            inth_sb[:, cl, 128 * t:128 * (t + 1)],
                                            wop[:, cl, 512 * hc:512 * (hc + 1)],
                                            start=False, stop=False)
                        tensor.ldweights(intl_sb[:, cl, 128 * t:128 * (t + 1)])
                        for hc in range(4):
                            ins = mm_nl(tensor, ps_d[:, 512 * hc:512 * (hc + 1)],
                                        intl_sb[:, cl, 128 * t:128 * (t + 1)],
                                        dWh_sb[:, cl, 512 * hc:512 * (hc + 1)],
                                        start=False, stop=True)
                        return ins
                    md = em("tensor", fn_md1,
                            waits=[("gp", ST["v_dd"][Bk * 2 + 1]),
                                   ("gp", ST["int_cur"][t // 4])], inc="mm")
                    ST["down_done"] = md
                    if t == NT - 1:
                        int_free = [md, md]

                    pump(1)

                    def fn_acc(vector, t=t, Bk=Bk):
                        if Bk == 1:
                            return vector.tensor_copy(out=acc_sb[:, t, :], in_=ps_d[:])
                        return vector.tensor_tensor(out=acc_sb[:, t, :], in0=acc_sb[:, t, :],
                                                    in1=ps_d[:], op=ALU.add)
                    ps_d_free = em("vector", fn_acc, waits=[("mm", md)], inc="ve")
                    pump(1)
            pump(99)
            if d1_blocks < NBLK:
                if "acc" in D:
                    def fn_dacc1(sync):
                        return sync.dma_start(out=D["acc"].rearrange("(a p) h -> p a h", p=128), in_=acc_sb[:])
                    em("sync", fn_dacc1, waits=[("ve", bb.c["ve"])], inc="st", incby=16)
                em("gpsimd", lambda g: g.nop(), waits=[("st", bb.c["st"])])
            bb.ph.realize(blk)
        for cm in reversed(psD1_cm):
            cm.__exit__(None, None, None)
        for cm in reversed(d1_cms):
            cm.__exit__(None, None, None)
        if d1_blocks < NBLK:
            return nc, D

        e_cms = []
        def _se(name, shape, dt=F32):
            cm = nc.sbuf_tensor(name, shape, dt)
            e_cms.append(cm)
            return cm.__enter__()
        x2e_sb = _se("x2e", [128, 2, H])
        ye_sb = _se("ye", [128, 2, H])
        with nc.Block("phE") as blk:
            bb.ph = Ph()
            x2slot = [0, 0]
            ystore = [0, 0]
            for t in range(NT):
                def fn_lde(sync, t=t):
                    return sync.dma_start(out=x2e_sb[:, t % 2, :],
                                          in_=hx2.rearrange("(a p) h -> p a h", p=128)[:, t, :])
                ld_e = em("sync", fn_lde,
                          waits=[("st0", st_after_c[0]), ("st1", st_after_c[1]),
                                 ("ve", x2slot[t % 2])],
                          inc="xd%d" % (t % 2), incby=16)

                def fn_y(vector, t=t):
                    x2ap = bass.AP(x2e_sb, (t % 2) * H, [[2 * H, 128], [1, 2], [2, 1024]])
                    yap = bass.AP(ye_sb, (t % 2) * H, [[2 * H, 128], [1, 2], [2, 1024]])
                    return vector.tensor_tensor(out=yap, in0=acc_sb[:, t, :], in1=x2ap, op=ALU.add)
                v_y = em("vector", fn_y, waits=[("xd%d" % (t % 2), ld_e),
                                                ("st%d" % (t % 2), ystore[t % 2])], inc="ve")
                x2slot[t % 2] = v_y

                def fn_sty(sync, t=t):
                    return sync.dma_start(out=hy.rearrange("(a p) h -> p a h", p=128)[:, t, :],
                                          in_=ye_sb[:, t % 2, :])
                ystore[t % 2] = em("sync", fn_sty, waits=[("ve", v_y)],
                                   inc="st%d" % (t % 2), incby=16)
            if "acc" in D:
                def fn_dacc(sync):
                    return sync.dma_start(out=D["acc"].rearrange("(a p) h -> p a h", p=128), in_=acc_sb[:])
                em("sync", fn_dacc, waits=[("ve", bb.c["ve"])], inc="st", incby=16)
            em("gpsimd", lambda g: g.nop(),
               waits=[("st", bb.c["st"]), ("st0", bb.c["st0"]), ("st1", bb.c["st1"])])
            bb.ph.realize(blk)
        for cm in reversed(e_cms):
            cm.__exit__(None, None, None)
        return nc, D


_NC_CACHE = {}


def kernel(**inputs):
    in_maps = _host_prep(inputs)
    if "main" not in _NC_CACHE:
        _NC_CACHE["main"] = build_nc(dbg=(), upto="E")
    nc, _ = _NC_CACHE["main"]
    res = run_bass_kernel_spmd(nc, in_maps, core_ids=list(range(8)))
    out = np.empty((4, 2 * T, H), np.float32)
    for core in range(8):
        b, half = divmod(core, 2)
        out[b, T * half:T * (half + 1)] = np.asarray(res.results[core]["y"])
    return out


# version 15
# speedup vs baseline: 1.1415x; 1.1415x over previous
import math
import numpy as np
import concourse.bass as bass
import concourse.mybir as mybir
from concourse.bass_utils import run_bass_kernel_spmd

F32 = mybir.dt.float32
F16 = mybir.dt.float16
U16 = mybir.dt.uint16
U8 = mybir.dt.uint8
AF = mybir.ActivationFunctionType
ALU = mybir.AluOpType

H = 2048
NK = 128
T = 1024
NT = 8
C = 64
NMLP = 16384
WN = 128
NBLK = NMLP // WN
EPS = 1e-5
SCALE = NK ** -0.5
LNS = math.log(SCALE)



def _bytes_of(packed_i32):
    a = np.ascontiguousarray(packed_i32)
    return np.ascontiguousarray(a.view(np.uint8).reshape(a.shape[0], -1)[:, ::4])


def _npp_order():
    tp = np.arange(64)[:, None, None]
    par = np.arange(2)[None, :, None]
    p = np.arange(128)[None, None, :]
    jp = 128 * tp + p
    j = (jp % 512) * 16 + jp // 512
    n = 2 * j + par
    return n.reshape(-1)


def _host_prep(inputs):
    x = np.asarray(inputs["L0_attn_hidden_states"], np.float32)
    wq = np.asarray(inputs["L0_attn_Wq"], np.float32)
    wk = np.asarray(inputs["L0_attn_Wk"], np.float32)
    wog = np.asarray(inputs["L0_attn_Wog"], np.float32)
    wig = np.asarray(inputs["L0_attn_Wig"], np.float32)
    gwv = np.asarray(inputs["L0_attn_g_weight"], np.float32)

    wq_t = np.ascontiguousarray(wq.T)
    wk_t = np.ascontiguousarray(wk.T)

    def _wblk(w):
        wt = np.ascontiguousarray(w.T)
        return np.ascontiguousarray(wt.reshape(16, 128, 4, 512).transpose(2, 0, 1, 3))

    wog_b = _wblk(wog)
    wig_b = _wblk(wig)
    gw_full = np.ascontiguousarray(np.broadcast_to(gwv[None, :], (128, H)))
    eye = np.eye(128, dtype=np.float32)
    mask = np.triu(np.ones((C, C), dtype=np.uint8))

    npp = _npp_order()

    def _gu_prep(packed, scales):
        b = _bytes_of(packed)
        bt = np.ascontiguousarray(b.T)
        jj = np.arange(1024)
        inv = (jj % 64) * 16 + jj // 64
        bt = bt[inv, :]
        m = np.arange(NMLP) % WN
        base = (np.arange(NMLP) // WN) * WN
        strip = np.where(m % 2 == 0, m // 2, WN // 2 + m // 2)
        cols = npp[base + strip]
        bt = np.ascontiguousarray(bt[:, cols])
        blk = np.ascontiguousarray(bt.reshape(8, 128, NBLK, WN).transpose(2, 1, 0, 3))
        st = np.ascontiguousarray(np.asarray(scales, np.float32).T)
        st = st[:, npp]
        stb = np.ascontiguousarray(st.reshape(64, NBLK, WN).transpose(1, 0, 2))
        return np.ascontiguousarray(blk.reshape(NBLK, 128, 8 * WN)).view('<u2'), stb

    gate_b, gate_s = _gu_prep(inputs["L0_ffn_gate_W_packed"], inputs["L0_ffn_gate_scales"])
    up_b, up_s = _gu_prep(inputs["L0_ffn_up_W_packed"], inputs["L0_ffn_up_scales"])

    db = _bytes_of(np.asarray(inputs["L0_ffn_down_W_packed"]))
    dbt = np.ascontiguousarray(db.T)
    jj = np.arange(8192)
    inv = (jj % 512) * 16 + jj // 512
    dbt = dbt[inv, :]
    down_b = np.ascontiguousarray(dbt.reshape(64, 128, H)).view('<u2')
    hpp = np.r_[0:H:2, 1:H:2]
    ds = np.ascontiguousarray(np.asarray(inputs["L0_ffn_down_scales"], np.float32).T)
    ds = ds[:, hpp]
    down_s = np.ascontiguousarray(ds.reshape(4, 128, H))

    shared = {
        "wq_t": wq_t, "wk_t": wk_t, "wog_b": wog_b, "wig_b": wig_b,
        "gw": gw_full, "eye": eye, "mask": mask,
        "gate_b": gate_b, "gate_s": gate_s, "up_b": up_b, "up_s": up_s,
        "down_b": down_b, "down_s": down_s,
    }
    zeros_half = np.zeros((T, H), np.float32)
    in_maps = []
    for core in range(8):
        b, half = divmod(core, 2)
        m = dict(shared)
        m["x_own"] = np.ascontiguousarray(x[b, T * half:T * (half + 1)])
        m["x_pref"] = np.ascontiguousarray(x[b, 0:T]) if half == 1 else zeros_half
        in_maps.append(m)
    return in_maps



class Ph:
    ENGS = ("sync", "tensor", "vector", "scalar", "gpsimd")

    def __init__(self):
        self.ops = {e: [] for e in self.ENGS}

    def realize(self, block):
        for e in self.ENGS:
            fns = self.ops[e]
            if fns:
                def body(eng, fns=tuple(fns)):
                    for f in fns:
                        f(eng)
                getattr(block, e)(body)


class B:

    def __init__(self, nc):
        self.nc = nc
        self.h = {}
        self.c = {}
        self.ph = None
        for n in ("ld", "sh", "st", "mm", "ve", "ac", "gp",
                  "wc0", "wc1", "wc2", "wc3", "xd0", "xd1", "ldg", "ldd", "st0", "st1"):
            cm = nc.semaphore(n)
            self.h[n] = cm.__enter__()
            self.c[n] = 0

    def em(self, eng, fn, waits=(), inc=None, incby=1, drain=None):
        if drain is None:
            drain = eng in ("vector", "scalar", "gpsimd")
        tgt = None
        if inc is not None:
            self.c[inc] += incby
            tgt = self.c[inc]
        waits = tuple((self.h[n], v) for (n, v) in waits if v is not None)

        def closure(engine, fn=fn, waits=waits, inc=inc, tgt=tgt, incby=incby, drain=drain):
            for (hh, vv) in waits:
                engine.wait_ge(hh, vv)
            ins = fn(engine)
            if drain:
                ins = engine.drain()
            if inc is not None:
                ins.then_inc(self.h[inc], incby)
        self.ph.ops[eng].append(closure)
        return tgt


def build_nc(dbg=(), upto="E", d1_blocks=NBLK):
    nc = bass.Bass("TRN2", target_bir_lowering=False, dynamic_dma_scratch_size=2048)
    bb = B(nc)

    def mm_nl(tensor, out, lhsT, rhs, start, stop):
        return tensor.add_instruction(mybir.InstMatmult(
            name=nc.get_next_instruction_name(),
            ins=[tensor.lower_ap(rhs), tensor.lower_ap(lhsT)],
            outs=[tensor.lower_ap(out)],
            replication_resolution=0, replication_shift_amnt=0,
            replication_num_rows=0,
            start_tensor_calc=start, stop_tensor_calc=stop, ldweights=False))

    hx = nc.dram_tensor("x_own", [T, H], F32, kind="ExternalInput").ap()
    hxp = nc.dram_tensor("x_pref", [T, H], F32, kind="ExternalInput").ap()
    hwq = nc.dram_tensor("wq_t", [H, NK], F32, kind="ExternalInput").ap()
    hwk = nc.dram_tensor("wk_t", [H, NK], F32, kind="ExternalInput").ap()
    hwog = nc.dram_tensor("wog_b", [4, 16, 128, 512], F32, kind="ExternalInput").ap()
    hwig = nc.dram_tensor("wig_b", [4, 16, 128, 512], F32, kind="ExternalInput").ap()
    hgw = nc.dram_tensor("gw", [128, H], F32, kind="ExternalInput").ap()
    heye = nc.dram_tensor("eye", [128, 128], F32, kind="ExternalInput").ap()
    hmask = nc.dram_tensor("mask", [C, C], U8, kind="ExternalInput").ap()
    hgb = nc.dram_tensor("gate_b", [NBLK, 128, 4 * WN], U16, kind="ExternalInput").ap()
    hgs = nc.dram_tensor("gate_s", [NBLK, 64, WN], F32, kind="ExternalInput").ap()
    hub = nc.dram_tensor("up_b", [NBLK, 128, 4 * WN], U16, kind="ExternalInput").ap()
    hus = nc.dram_tensor("up_s", [NBLK, 64, WN], F32, kind="ExternalInput").ap()
    hdb = nc.dram_tensor("down_b", [64, 128, H // 2], U16, kind="ExternalInput").ap()
    hds = nc.dram_tensor("down_s", [4, 128, H], F32, kind="ExternalInput").ap()
    hy = nc.dram_tensor("y", [T, H], F32, kind="ExternalOutput").ap()
    hx2 = nc.dram_tensor("x2_scratch", [T, H], F32).ap()

    D = {}

    def dbg_t(name, shape):
        if name in dbg:
            D[name] = nc.dram_tensor("dbg_" + name, shape, F32, kind="ExternalOutput").ap()

    dbg_t("sgo", [T, H]); dbg_t("attn", [T, H]); dbg_t("rms", [128, NT])
    dbg_t("ln2t", [128, 16 * T])
    dbg_t("o", [T, H]); dbg_t("q", [NK, T]); dbg_t("k", [NK, 2 * T])
    dbg_t("ct", [NK, 2 * T]); dbg_t("S", [NK, H]); dbg_t("x2", [T, H])
    dbg_t("acc", [T, H]); dbg_t("wg", [128, 16 * WN]); dbg_t("wd", [128, 2 * H])
    dbg_t("inter", [256, T])

    big0 = nc.alloc_sbuf_tensor("big0", [128, NT * H], F32)
    big1 = nc.alloc_sbuf_tensor("big1", [128, NT * H], F32)
    x_sb = big0.ap().rearrange("p (a h) -> p a h", a=NT)
    xp_sb = big1.ap().rearrange("p (a h) -> p a h", a=NT)
    o_sb = xp_sb
    ln2t_sb = big0.ap().rearrange("p (a t) -> p a t", a=16)
    ln2t16 = big0.ap().bitcast(F16).rearrange("p (c a t) -> p c a t", c=2, a=16)
    acc_sb = xp_sb

    early_cms = []

    def _sb(name, shape, dt=F32):
        cm = nc.sbuf_tensor(name, shape, dt)
        early_cms.append(cm)
        return cm.__enter__()

    if True:
        eye_sb = _sb("eyet", [128, 128])
        mask_sb = _sb("maskt", [C, C], U8)
        st1 = _sb("stat1", [128, NT, 2])
        st1p = _sb("stat1p", [128, NT, 2])
        st2 = _sb("stat2", [128, NT, 2])
        rms_sb = _sb("rms2", [128, NT])
        bn_sb = _sb("bnb", [128, 16, 27])
        zc_sb = _sb("zcol", [128, 2])
        cst_sb = _sb("cstc", [128, 4])
        ab_cms = []
        def _sba(name, shape, dt=F32):
            cm = nc.sbuf_tensor(name, shape, dt)
            ab_cms.append(cm)
            return cm.__enter__()
        gw_sb = _sba("gwt", [128, H])
        q_sb = _sba("qbuf", [128, T])
        k_sb = _sba("kbuf", [128, 2 * T])
        ct_sb = _sba("ctbuf", [128, 2 * T])
        a_cms = []
        def _sbq(name, shape, dt=F32):
            cm = nc.sbuf_tensor(name, shape, dt)
            a_cms.append(cm)
            return cm.__enter__()
        wq_sb = _sbq("wq_sb", [128, 16, NK])
        wk_sb = _sbq("wk_sb", [128, 16, NK])
        mov_sb = _sbq("mov", [128, 2, 512])
        t_scr = _sbq("tscr", [128, 128])
        sp_sb = _sbq("spbuf", [128, 512])
        sp2_sb = _sbq("spbuf2", [128, 512])
        ones_sb = _sbq("onesb", [128, 512])
        with nc.Block("phAB") as blk:
            bb.ph = Ph()
            em = bb.em
            psA_cm = [nc.psum_tensor("psq", [128, 512], F32),
                      nc.psum_tensor("psk", [128, 512], F32),
                      nc.psum_tensor("pstr", [128, 1024], F32)]
            ps_q, ps_k, ps_t = [cm.__enter__() for cm in psA_cm]

            def dma_in(sync):
                ins = None
                for out, src in (
                    (x_sb[:], hx.rearrange("(a p) h -> p a h", p=128)),
                    (xp_sb[:], hxp.rearrange("(a p) h -> p a h", p=128)),
                    (wq_sb[:], hwq.rearrange("(a p) k -> p a k", p=128)),
                    (wk_sb[:], hwk.rearrange("(a p) k -> p a k", p=128)),
                    (gw_sb[:], hgw[:]),
                    (eye_sb[:], heye[:]),
                    (mask_sb[:], hmask[:]),
                ):
                    ins = sync.dma_start(out=out, in_=src)
                    ins.then_inc(bb.h["ld"], 16)
                return ins
            bb.ph.ops["sync"].append(lambda sync: dma_in(sync))
            bb.c["ld"] += 7 * 16
            n_in = bb.c["ld"]

            def ln1_stats(vector):
                vector.memset(zc_sb[:], 0.0)
                vector.memset(cst_sb[:, 0:1], -1.0)
                vector.memset(cst_sb[:, 1:2], LNS)
                vector.memset(cst_sb[:, 2:3], 1.0 / H)
                vector.memset(cst_sb[:, 3:4], EPS)
                vector.memset(ones_sb[:], 1.0)
                for i in range(16):
                    src = x_sb if i < 8 else xp_sb
                    for s4 in range(4):
                        vector.bn_stats(out=bn_sb[:, i, 6 * s4:6 * s4 + 6],
                                        in_=src[:, i % 8, 512 * s4:512 * (s4 + 1)])
                vector.drain()
                for i in range(16):
                    vector.bn_aggr(out=bn_sb[:, i, 24:26], in_=bn_sb[:, i, 0:24])
                vector.drain()
                for i in range(16):
                    dst = st1 if i < 8 else st1p
                    vector.tensor_copy(out=dst[:, i % 8, 0:1], in_=bn_sb[:, i, 24:25])
                ins = None
                for i in range(16):
                    ins = vector.tensor_scalar(out=bn_sb[:, i, 26:27], in0=bn_sb[:, i, 25:26],
                                               scalar1=1.0, scalar2=EPS, op0=ALU.mult, op1=ALU.add)
                return ins
            v1 = em("vector", ln1_stats, waits=[("ld", n_in)], inc="ve")

            def ln1_sqrt(scalar):
                ins = None
                for i in range(16):
                    ins = scalar.activation(out=bn_sb[:, i, 26:27], in_=bn_sb[:, i, 26:27], func=AF.Sqrt)
                return ins
            a1 = em("scalar", ln1_sqrt, waits=[("ve", v1)], inc="ac")

            def ln1_recip(vector):
                ins = None
                for i in range(16):
                    dst = st1 if i < 8 else st1p
                    ins = vector.reciprocal(out=dst[:, i % 8, 1:2], in_=bn_sb[:, i, 26:27])
                return ins
            v_ln = em("vector", ln1_recip, waits=[("ac", a1)], inc="ve")

            quarter_free = [0, 0, 0, 0]
            slot_free = [0, 0]
            last_tr = 0
            last_evac = [0, 0]
            qk_done = []
            for g in range(4):
                own = g >= 2
                src = x_sb if own else xp_sb
                stat = st1 if own else st1p
                tbase = (g % 2) * 4
                for h in range(16):
                    slot = h % 2
                    v_ev = 0
                    for tt in range(4):
                        qcol = (tt % 2) * 512 + (tt // 2) * 256

                        def ts_slice(vector, j=tbase + tt, h=h, src=src, stat=stat):
                            return vector.tensor_scalar(
                                out=t_scr[:], in0=src[:, j, 128 * h:128 * (h + 1)],
                                scalar1=stat[:, j, 0:1], scalar2=stat[:, j, 1:2],
                                op0=ALU.subtract, op1=ALU.mult)
                        w = [("ve", v_ln if (g, h, tt) == (0, 0, 0) else 0), ("mm", last_tr)]
                        v_ts = em("vector", ts_slice, waits=w, inc="ve")

                        def transp(tensor, qcol=qcol):
                            return tensor.transpose(ps_t[:, qcol:qcol + 128], t_scr[:], eye_sb[:])
                        last_tr = em("tensor", transp,
                                     waits=[("ve", v_ts), ("ve", quarter_free[tt])], inc="mm")

                        def evac(vector, tt=tt, slot=slot, qcol=qcol):
                            return vector.tensor_copy(out=mov_sb[:, slot, 128 * tt:128 * (tt + 1)],
                                                      in_=ps_t[:, qcol:qcol + 128])
                        wv = [("mm", last_tr)]
                        if tt == 0:
                            wv.append(("mm", slot_free[slot]))
                        v_ev = em("vector", evac, waits=wv, inc="ve")
                        quarter_free[tt] = v_ev

                    def mm_qk(tensor, g=g, h=h, slot=slot, own=own):
                        if own:
                            tensor.matmul(ps_q[:], wq_sb[:, h, :], mov_sb[:, slot, :],
                                          start=(h == 0), stop=(h == 15))
                        return tensor.matmul(ps_k[:], wk_sb[:, h, :], mov_sb[:, slot, :],
                                             start=(h == 0), stop=(h == 15))
                    wmm = [("ve", v_ev)]
                    if h == 0:
                        wmm += [("ac", last_evac[0]), ("ac", last_evac[1])]
                    m_qk = em("tensor", mm_qk, waits=wmm, inc="mm")
                    slot_free[slot] = m_qk

                kcols = slice(512 * g, 512 * (g + 1))

                def ev_k(scalar, kcols=kcols):
                    return scalar.activation(out=k_sb[:, kcols], in_=ps_k[:], func=AF.Sigmoid)
                a_k = em("scalar", ev_k, waits=[("mm", m_qk)], inc="ac")
                last_evac[0] = a_k

                if own:
                    def ev_q(scalar, qcols=slice(512 * (g - 2), 512 * (g - 1))):
                        return scalar.activation(out=q_sb[:, qcols], in_=ps_q[:], func=AF.Silu)
                    a_q = em("scalar", ev_q, inc="ac")
                    last_evac[1] = a_q

                def ev_sp(scalar, kcols=kcols):
                    scalar.drain()
                    scalar.activation(out=sp_sb[:], in_=k_sb[:, kcols], func=AF.Sigmoid,
                                      scale=cst_sb[:, 0:1])
                    scalar.drain()
                    return scalar.activation(out=sp2_sb[:], in_=sp_sb[:], func=AF.Ln)
                a_sp = em("scalar", ev_sp, inc="ac")

                def scan(vector, g=g, kcols=kcols):
                    vector.drain()
                    init = 0.0 if g == 0 else ct_sb[:, 512 * g - 1:512 * g]
                    return vector.tensor_tensor_scan(out=ct_sb[:, kcols], data0=ones_sb[:],
                                                     data1=sp2_sb[:], initial=init,
                                                     op0=ALU.mult, op1=ALU.add)
                v_sc = em("vector", scan, waits=[("ac", a_sp)], inc="ve")
            v_ct = v_sc

            bb.ph.realize(blk)
        for cm in reversed(psA_cm):
            cm.__exit__(None, None, None)
        for cm in reversed(a_cms):
            cm.__exit__(None, None, None)
        b_cms = []
        def _sbb(name, shape, dt=F32):
            cm = nc.sbuf_tensor(name, shape, dt)
            b_cms.append(cm)
            return cm.__enter__()
        S_sb = _sbb("sbufS", [128, H])
        ln_scr = _sbb("lnscr", [128, H])
        v_scr = _sbb("vscr", [64, H])
        sp_sb = _sbb("spbufB", [128, 512])
        e1i = _sbb("e1i", [128, C])
        e2i = _sbb("e2i", [128, C])
        expq = _sbb("expq", [128, C])
        expk = _sbb("expk", [128, C])
        exp2 = _sbb("exp2", [128, C])
        dc_sb = _sbb("dcol", [128, 2])
        qt_sb = _sbb("qt", [128, C])
        kt_sb = _sbb("kt", [128, C])
        kh_sb = _sbb("khat", [128, C])
        ktT_sb = _sbb("ktT", [64, 128])
        pt_sb = _sbb("ptsb", [64, C])
        psB_cm = [nc.psum_tensor("psa", [128, H], F32),
                  nc.psum_tensor("pso", [128, H], F32)]
        ps_a, ps_o = [cm.__enter__() for cm in psB_cm]
        with nc.Block("phB") as blk:
            bb.ph = Ph()
            v_S0 = em("vector", lambda v: v.memset(S_sb[:], 0.0), inc="ve")

            t1_prev = t2_prev = t3_prev = t4_prev = 0
            v3_prev = v4_prev = v5_prev = 0
            v6_prev = v_S0
            sh_prev = 0
            v_ln_scr = 0
            s1_prev = 0
            v5_list = []
            for ch in range(32):
                own = ch >= 16
                par = ch % 2
                tile = (ch % 16) // 2
                src = x_sb if own else xp_sb
                stat = st1 if own else st1p
                c0 = C * ch
                oc0 = C * (ch - 16)

                def fn_v1(vector, ch=ch, own=own, par=par, tile=tile, src=src, stat=stat, c0=c0):
                    vector.tensor_scalar(out=e2i[:], in0=ct_sb[:, c0:c0 + C],
                                         scalar1=ct_sb[:, c0 + C - 1:c0 + C], scalar2=None,
                                         op0=ALU.subtract)
                    if own:
                        vector.tensor_scalar(out=e1i[:], in0=ct_sb[:, c0:c0 + C],
                                             scalar1=(ct_sb[:, c0 - 1:c0] if ch > 0 else zc_sb[:, 0:1]),
                                             scalar2=None, op0=ALU.subtract)
                    prev = ct_sb[:, c0 - 1:c0] if ch > 0 else zc_sb[:, 0:1]
                    ins = vector.tensor_scalar(out=dc_sb[:, 0:1], in0=prev,
                                               scalar1=ct_sb[:, c0 + C - 1:c0 + C], scalar2=None,
                                               op0=ALU.subtract)
                    if par == 0:
                        ins = vector.tensor_scalar(out=ln_scr[:], in0=src[:, tile, :],
                                                   scalar1=stat[:, tile, 0:1], scalar2=stat[:, tile, 1:2],
                                                   op0=ALU.subtract, op1=ALU.mult)
                    return ins
                wv1 = [("ac", s1_prev), ("ve", v_ct if ch == 0 else 0)]
                if par == 0 and ch >= 2:
                    wv1 += [("sh", sh_prev), ("mm", t4_prev)]
                v1c = em("vector", fn_v1, waits=wv1, inc="ve")
                if par == 0:
                    v_ln_scr = v1c

                def fn_s1(scalar, own=own):
                    scalar.activation(out=exp2[:], in_=e2i[:], func=AF.Exp, scale=cst_sb[:, 0:1])
                    if own:
                        scalar.activation(out=expq[:], in_=e1i[:], func=AF.Exp, bias=cst_sb[:, 1:2])
                        scalar.activation(out=expk[:], in_=e1i[:], func=AF.Exp, scale=cst_sb[:, 0:1])
                    return scalar.activation(out=dc_sb[:, 1:2], in_=dc_sb[:, 0:1], func=AF.Exp,
                                             scale=cst_sb[:, 0:1])
                s1c = em("scalar", fn_s1, waits=[("ve", v1c)], inc="ac")
                s1_prev = s1c

                def fn_v2(vector, own=own, c0=c0, oc0=oc0):
                    ins = vector.tensor_tensor(out=kh_sb[:], in0=k_sb[:, c0:c0 + C], in1=exp2[:], op=ALU.mult)
                    if own:
                        vector.tensor_tensor(out=qt_sb[:], in0=q_sb[:, oc0:oc0 + C], in1=expq[:], op=ALU.mult)
                        ins = vector.tensor_tensor(out=kt_sb[:], in0=k_sb[:, c0:c0 + C], in1=expk[:], op=ALU.mult)
                    return ins
                v2c = em("vector", fn_v2,
                         waits=[("ac", s1c), ("mm", t1_prev), ("mm", t2_prev)], inc="ve")

                if par == 1:
                    def fn_sh(scalar):
                        return scalar.dma_start(out=v_scr[:], in_=ln_scr[64:128, :])
                    sh_prev = em("scalar", fn_sh, waits=[("ve", v_ln_scr), ("mm", t4_prev)],
                                 inc="sh", incby=16, drain=False)

                def fn_t1(tensor):
                    return tensor.transpose(ps_a[0:64, 0:128], kh_sb[:], eye_sb[:])
                t1c = em("tensor", fn_t1, waits=[("ve", v2c), ("ve", v6_prev)], inc="mm")
                t1_prev = t1c

                v3c = em("vector", lambda v: v.tensor_copy(out=ktT_sb[:], in_=ps_a[0:64, 0:128]),
                         waits=[("mm", t1c)], inc="ve")
                v3_prev = v3c

                if own:
                    def fn_t2(tensor):
                        return tensor.matmul(ps_a[0:64, 0:C], kt_sb[:], qt_sb[:], start=True, stop=True)
                    t2c = em("tensor", fn_t2, waits=[("ve", v3c)], inc="mm")
                    t2_prev = t2c

                    def fn_v4(vector):
                        vector.memset(pt_sb[:], 0.0)
                        vector.drain()
                        return vector.copy_predicated(out=pt_sb[:], mask=mask_sb[:], data=ps_a[0:64, 0:C])
                    v4c = em("vector", fn_v4, waits=[("mm", t2c)], inc="ve")
                    v4_prev = v4c

                    vsel = v_scr if par == 1 else ln_scr
                    def fn_t3(tensor, par=par, vsel=vsel):
                        tp = (0, 64 * par)
                        r = slice(64 * par, 64 * par + 64)
                        ins = None
                        for hc in range(4):
                            cols = slice(512 * hc, 512 * (hc + 1))
                            tensor.matmul(ps_o[r, cols], qt_sb[:], S_sb[:, cols],
                                          start=True, stop=False, tile_position=tp)
                            ins = tensor.matmul(ps_o[r, cols], pt_sb[:], vsel[0:64, cols],
                                                start=False, stop=True, tile_position=tp)
                        return ins
                    wt3 = [("ve", v4c), ("ve", v6_prev), ("ve", v5_prev)]
                    wt3.append(("sh", sh_prev) if par == 1 else ("ve", v_ln_scr))
                    t3c = em("tensor", fn_t3, waits=wt3, inc="mm")
                    t3_prev = t3c

                    def fn_v5(vector, par=par, tile=tile):
                        r = slice(64 * par, 64 * par + 64)
                        return vector.tensor_copy(out=o_sb[r, tile, :], in_=ps_o[r, :])
                    v5c = em("vector", fn_v5, waits=[("mm", t3c)], inc="ve")
                    v5_prev = v5c
                    v5_list.append(v5c)

                vsel = v_scr if par == 1 else ln_scr
                def fn_t4(tensor, vsel=vsel):
                    ins = None
                    for hc in range(4):
                        cols = slice(512 * hc, 512 * (hc + 1))
                        ins = tensor.matmul(ps_a[:, cols], ktT_sb[:], vsel[0:64, cols],
                                            start=True, stop=True)
                    return ins
                wt4 = [("ve", v3c), ("ve", v4_prev if own else 0)]
                wt4.append(("sh", sh_prev) if par == 1 else ("ve", v_ln_scr))
                if own:
                    wt4.append(("mm", t3c))
                t4c = em("tensor", fn_t4, waits=wt4, inc="mm")
                t4_prev = t4c

                def fn_v6(vector):
                    return vector.scalar_tensor_tensor(out=S_sb[:], in0=S_sb[:], scalar=dc_sb[:, 1:2],
                                                       in1=ps_a[:], op0=ALU.mult, op1=ALU.add)
                v6c = em("vector", fn_v6, waits=[("mm", t4c), ("ac", s1c)], inc="ve")
                v6_prev = v6c

            dbg_srcs = {"q": q_sb[:], "k": k_sb[:], "ct": ct_sb[:], "S": S_sb[:]}
            for name in ("q", "k", "ct", "S", "o"):
                if name not in D:
                    continue
                if name == "o":
                    pair = (D["o"].rearrange("(a p) h -> p a h", p=128), o_sb[:])
                else:
                    pair = (D[name], dbg_srcs[name])
                def fn_dbg(sync, pair=pair):
                    return sync.dma_start(out=pair[0], in_=pair[1])
                em("sync", fn_dbg, waits=[("ve", bb.c["ve"]), ("ac", bb.c["ac"])],
                   inc="st", incby=16)
            if upto == "B":
                em("gpsimd", lambda g: g.nop(), waits=[("st", bb.c["st"])])
            bb.ph.realize(blk)
        for cm in reversed(psB_cm):
            cm.__exit__(None, None, None)
        for cm in reversed(b_cms):
            cm.__exit__(None, None, None)
        if upto == "B":
            for cm in reversed(ab_cms):
                cm.__exit__(None, None, None)
            return nc, D

        c_cms = []
        def _sbc(name, shape, dt=F32):
            cm = nc.sbuf_tensor(name, shape, dt)
            c_cms.append(cm)
            return cm.__enter__()
        wqt_sb = _sbc("wquart", [128, 4, 512])
        w16h_c = _sbc("w16h", [128, 4, 512], F16)
        w16l_c = _sbc("w16l", [128, 4, 512], F16)
        t4_scr = _sbc("t4scr", [128, 2, 512])
        sth_c = _sbc("sthc", [128, 2, 4, 128], F16)
        stl_c = _sbc("stlc", [128, 2, 4, 128], F16)
        sgo_sb = _sbc("sgo", [128, 512])
        ep1_sb = _sbc("ep1", [128, 512])
        ep2_sb = _sbc("ep2", [128, 512])
        ysc_sb = _sbc("yscr", [128, 2, 512])
        sq_scr = _sbc("sqscr", [128, H])
        psC_cm = [nc.psum_tensor("psc", [128, H], F32),
                  nc.psum_tensor("psctr", [128, 2, 512], F32)]
        ps_c, ps_ctr = [cm.__enter__() for cm in psC_cm]

        with nc.Block("phC") as blk:
            bb.ph = Ph()

            def c_rms1(vector):
                ins = None
                for t in range(NT):
                    vector.tensor_tensor(out=sq_scr[:], in0=o_sb[:, t, :], in1=o_sb[:, t, :], op=ALU.mult)
                    vector.drain()
                    ins = vector.tensor_reduce(out=bn_sb[:, t, 0:1], in_=sq_scr[:], op=ALU.add,
                                               axis=mybir.AxisListType.X)
                    vector.drain()
                return ins
            vr1 = em("vector", c_rms1, inc="ve")

            def c_rms2(scalar):
                ins = None
                for t in range(NT):
                    ins = scalar.activation(out=bn_sb[:, t, 1:2], in_=bn_sb[:, t, 0:1], func=AF.Sqrt,
                                            scale=cst_sb[:, 2:3], bias=cst_sb[:, 3:4])
                return ins
            ar2 = em("scalar", c_rms2, waits=[("ve", vr1)], inc="ac")

            def c_rms3(vector):
                ins = None
                for t in range(NT):
                    ins = vector.reciprocal(out=rms_sb[:, t:t + 1], in_=bn_sb[:, t, 1:2])
                return ins
            vr3 = em("vector", c_rms3, waits=[("ac", ar2)], inc="ve")
            if "rms" in D:
                em("sync", lambda s: s.dma_start(out=D["rms"], in_=rms_sb[:]),
                   waits=[("ve", vr3)], inc="st", incby=16)

            wslot_free = [0, 0, 0, 0]
            quarter_free = [0, 0, 0, 0]
            grp_used = [0, 0]
            tr_done = [0, 0]
            evh_done = [0, 0]
            evl_done = [0, 0]
            y_free = [0, 0]
            sgo_used = 0
            sgo_sem = "st" if "sgo" in D else "ve"
            for q4 in range(4):
                qcols = slice(512 * q4, 512 * (q4 + 1))
                for tg in range(2):
                    mm_c_last = 0
                    pend_mm = None
                    for mat in range(2):
                        hw = hwog if mat == 0 else hwig
                        for kk in range(16):
                            slot = kk % 4
                            grp = kk % 2
                            wsem = "wc%d" % slot

                            def fn_wld(sync, hw=hw, q4=q4, kk=kk, slot=slot):
                                return sync.dma_start(out=wqt_sb[:, slot, :], in_=hw[q4, kk])
                            ld_c = em("sync", fn_wld, waits=[("mm", wslot_free[slot])],
                                      inc=wsem, incby=16)

                            def fn_wh(scalar, slot=slot):
                                return scalar.copy(out=w16h_c[:, slot, :], in_=wqt_sb[:, slot, :])
                            a_wh = em("scalar", fn_wh,
                                      waits=[(wsem, ld_c), ("mm", wslot_free[slot])], inc="ac")

                            def fn_wl(vector, slot=slot):
                                return vector.tensor_tensor(out=w16l_c[:, slot, :],
                                                            in0=wqt_sb[:, slot, :],
                                                            in1=w16h_c[:, slot, :], op=ALU.subtract)
                            v_wl = em("vector", fn_wl, waits=[("ac", a_wh)], inc="ve")

                            if mat == 1:
                                def fn_ts4(vector, tg=tg, kk=kk, grp=grp):
                                    ins = None
                                    for ti in range(4):
                                        ins = vector.tensor_scalar(
                                            out=t4_scr[:, grp, 128 * ti:128 * (ti + 1)],
                                            in0=x_sb[:, tg * 4 + ti, 128 * kk:128 * (kk + 1)],
                                            scalar1=st1[:, tg * 4 + ti, 0:1],
                                            scalar2=st1[:, tg * 4 + ti, 1:2],
                                            op0=ALU.subtract, op1=ALU.mult)
                                    return ins
                                v_ts = em("vector", fn_ts4, waits=[("mm", tr_done[grp])], inc="ve")

                            def fn_tr4(tensor, mat=mat, tg=tg, kk=kk, grp=grp):
                                ins = None
                                for ti in range(4):
                                    src_ap = (t4_scr[:, grp, 128 * ti:128 * (ti + 1)] if mat == 1
                                              else o_sb[:, tg * 4 + ti, 128 * kk:128 * (kk + 1)])
                                    ins = tensor.transpose(ps_ctr[:, grp, 128 * ti:128 * (ti + 1)],
                                                           src_ap, eye_sb[:])
                                return ins
                            trw = [("ac", evh_done[grp]), ("ve", evl_done[grp])]
                            if mat == 1:
                                trw.append(("ve", v_ts))
                            tr_done[grp] = em("tensor", fn_tr4, waits=trw, inc="mm")

                            def fn_eh4(scalar, grp=grp):
                                ins = None
                                for ti in range(4):
                                    ins = scalar.copy(out=sth_c[:, grp, ti, :],
                                                      in_=ps_ctr[:, grp, 128 * ti:128 * (ti + 1)])
                                return ins
                            evh_done[grp] = em("scalar", fn_eh4,
                                               waits=[("mm", tr_done[grp]), ("mm", grp_used[grp])],
                                               inc="ac")

                            def fn_el4(vector, grp=grp):
                                ins = None
                                for ti in range(4):
                                    ins = vector.tensor_tensor(out=stl_c[:, grp, ti, :],
                                                               in0=ps_ctr[:, grp, 128 * ti:128 * (ti + 1)],
                                                               in1=sth_c[:, grp, ti, :],
                                                               op=ALU.subtract)
                                return ins
                            evl_done[grp] = em("vector", fn_el4, waits=[("ac", evh_done[grp])],
                                               inc="ve")

                            def fn_mm4(tensor, grp=grp, slot=slot, mat=mat, kk=kk, tg=tg):
                                ins = None
                                for ti in range(4):
                                    tq = (tg * 4 + ti) % 4
                                    tensor.ldweights(sth_c[:, grp, ti, :])
                                    ins = mm_nl(tensor, ps_c[:, 512 * tq:512 * tq + 512],
                                                sth_c[:, grp, ti, :], w16h_c[:, slot, :],
                                                start=(mat == 0 and kk == 0), stop=False)
                                    ins = mm_nl(tensor, ps_c[:, 512 * tq:512 * tq + 512],
                                                sth_c[:, grp, ti, :], w16l_c[:, slot, :],
                                                start=False, stop=False)
                                    tensor.ldweights(stl_c[:, grp, ti, :])
                                    ins = mm_nl(tensor, ps_c[:, 512 * tq:512 * tq + 512],
                                                stl_c[:, grp, ti, :], w16h_c[:, slot, :],
                                                start=False, stop=(mat == 1 and kk == 15))
                                return ins
                            wmm = [("ve", evl_done[grp]), ("ve", v_wl)]
                            if mat == 0 and kk == 0:
                                wmm += [("ac", quarter_free[i]) for i in range(4)]
                            if pend_mm is not None:
                                pfn, pw, pslot, pgrp = pend_mm
                                mm_c_last = em("tensor", pfn, waits=pw, inc="mm")
                                grp_used[pgrp] = mm_c_last
                                wslot_free[pslot] = mm_c_last
                            pend_mm = (fn_mm4, wmm, slot, grp)
                    pfn, pw, pslot, pgrp = pend_mm
                    mm_c_last = em("tensor", pfn, waits=pw, inc="mm")
                    grp_used[pgrp] = mm_c_last
                    wslot_free[pslot] = mm_c_last

                    for ti in range(4):
                        t = tg * 4 + ti

                        def fn_si(scalar, t=t):
                            return scalar.activation(out=sgo_sb[:], in_=ps_c[:, 512 * (t % 4):512 * (t % 4) + 512],
                                                     func=AF.Silu)
                        a_si = em("scalar", fn_si, waits=[("mm", mm_c_last), (sgo_sem, sgo_used)], inc="ac")
                        quarter_free[t % 4] = a_si

                        def fn_e1(vector, t=t, qcols=qcols):
                            return vector.scalar_tensor_tensor(out=ep1_sb[:], in0=o_sb[:, t, qcols],
                                                               scalar=rms_sb[:, t:t + 1], in1=gw_sb[:, qcols],
                                                               op0=ALU.mult, op1=ALU.mult)
                        v_e1 = em("vector", fn_e1, waits=[("ve", vr3 if (q4, tg, ti) == (0, 0, 0) else 0)], inc="ve")

                        def fn_e2(vector):
                            return vector.tensor_tensor(out=ep2_sb[:], in0=ep1_sb[:], in1=sgo_sb[:], op=ALU.mult)
                        v_e2 = em("vector", fn_e2, waits=[("ac", a_si)], inc="ve")
                        sgo_used = v_e2
                        if "sgo" in D:
                            def fn_dsg(sync, t=t, q4=q4):
                                return sync.dma_start(
                                    out=D["sgo"].rearrange("(a p) h -> p a h", p=128)[:, t, 512 * q4:512 * (q4 + 1)],
                                    in_=sgo_sb[:])
                            sgo_used = em("sync", fn_dsg, waits=[("ve", v_e2)], inc="st", incby=16)
                        if "attn" in D:
                            def fn_dat(sync, t=t, q4=q4):
                                return sync.dma_start(
                                    out=D["attn"].rearrange("(a p) h -> p a h", p=128)[:, t, 512 * q4:512 * (q4 + 1)],
                                    in_=ep2_sb[:])
                            em("sync", fn_dat, waits=[("ve", v_e2)], inc="st", incby=16)

                        yslot = t % 2

                        def fn_e3(vector, t=t, qcols=qcols, yslot=yslot):
                            return vector.tensor_tensor(out=ysc_sb[:, yslot, :], in0=x_sb[:, t, qcols],
                                                        in1=ep2_sb[:], op=ALU.add)
                        v_e3 = em("vector", fn_e3, waits=[("st%d" % yslot, y_free[yslot])], inc="ve")

                        def fn_x2st(sync, t=t, q4=q4, yslot=yslot):
                            return sync.dma_start(
                                out=hx2.rearrange("(a p) h -> p a h", p=128)[:, t, 512 * q4:512 * (q4 + 1)],
                                in_=ysc_sb[:, yslot, :])
                        y_free[yslot] = em("sync", fn_x2st, waits=[("ve", v_e3)],
                                           inc="st%d" % yslot, incby=16)
            st_after_c = (bb.c["st0"], bb.c["st1"])
            if upto == "C":
                def fn_dx2(sync):
                    return sync.dma_start(out=D["x2"], in_=hx2[:])
                if "x2" in D:
                    em("sync", fn_dx2, waits=[("st", st_after_c)], inc="st", incby=16)
                em("gpsimd", lambda g: g.nop(), waits=[("st", bb.c["st"])])
            bb.ph.realize(blk)
        for cm in reversed(psC_cm):
            cm.__exit__(None, None, None)
        for cm in reversed(c_cms):
            cm.__exit__(None, None, None)
        for cm in reversed(ab_cms):
            cm.__exit__(None, None, None)
        if upto == "C":
            return nc, D

        d0_cms = []
        def _sd0(name, shape, dt=F32):
            cm = nc.sbuf_tensor(name, shape, dt)
            d0_cms.append(cm)
            return cm.__enter__()
        x2r_sb = _sd0("x2r", [128, 2, H])
        ln2s_sb = _sd0("ln2s", [128, H])
        g_scr = _sd0("gscr", [128, 128])
        psD0_cm = [nc.psum_tensor("psd0", [128, 512], F32)]
        ps_d0 = psD0_cm[0].__enter__()

        with nc.Block("phD0") as blk:
            bb.ph = Ph()
            ring_free = [0, 0]
            tr_last_d0 = 0
            ev_d0 = 0
            ev_hi_d0 = 0
            ln2_used = 0
            for t in range(NT):
                def fn_ldx2(sync, t=t):
                    return sync.dma_start(out=x2r_sb[:, t % 2, :],
                                          in_=hx2.rearrange("(a p) h -> p a h", p=128)[:, t, :])
                ld_x2 = em("sync", fn_ldx2,
                           waits=[("st0", st_after_c[0]), ("st1", st_after_c[1]),
                                  ("ve", ring_free[t % 2])],
                           inc="xd%d" % (t % 2), incby=16)

                def fn_bn(vector, t=t):
                    for s4 in range(4):
                        vector.bn_stats(out=bn_sb[:, t, 6 * s4:6 * s4 + 6],
                                        in_=x2r_sb[:, t % 2, 512 * s4:512 * (s4 + 1)])
                    vector.drain()
                    vector.bn_aggr(out=bn_sb[:, t, 24:26], in_=bn_sb[:, t, 0:24])
                    vector.drain()
                    vector.tensor_copy(out=st2[:, t, 0:1], in_=bn_sb[:, t, 24:25])
                    return vector.tensor_scalar(out=bn_sb[:, t, 26:27], in0=bn_sb[:, t, 25:26],
                                                scalar1=1.0, scalar2=EPS, op0=ALU.mult, op1=ALU.add)
                v_bn = em("vector", fn_bn, waits=[("xd%d" % (t % 2), ld_x2)], inc="ve")

                def fn_sq(scalar, t=t):
                    return scalar.activation(out=bn_sb[:, t, 26:27], in_=bn_sb[:, t, 26:27], func=AF.Sqrt)
                a_sq = em("scalar", fn_sq, waits=[("ve", v_bn)], inc="ac")

                def fn_rc(vector, t=t):
                    vector.reciprocal(out=st2[:, t, 1:2], in_=bn_sb[:, t, 26:27])
                    vector.drain()
                    return vector.tensor_scalar(out=ln2s_sb[:], in0=x2r_sb[:, t % 2, :],
                                                scalar1=st2[:, t, 0:1], scalar2=st2[:, t, 1:2],
                                                op0=ALU.subtract, op1=ALU.mult)
                v_ln2 = em("vector", fn_rc, waits=[("ac", a_sq)], inc="ve")
                ring_free[t % 2] = v_ln2

                for u in range(8):
                    for par in range(2):
                        def fn_gth(vector, u=u, par=par):
                            sel = bass.AP(ln2s_sb, 4 * u + par, [[2048, 128], [2, 2], [32, 64]])
                            return vector.tensor_copy(out=g_scr[:], in_=sel)
                        v_g = em("vector", fn_gth, waits=[("ve", v_ln2), ("mm", tr_last_d0)], inc="ve")
                        ln2_used = max(ln2_used, v_g)

                        def fn_trp(tensor):
                            return tensor.transpose(ps_d0[:, 0:128], g_scr[:], eye_sb[:])
                        tr_last_d0 = em("tensor", fn_trp,
                                        waits=[("ve", v_g), ("ve", ev_d0), ("ac", ev_hi_d0)], inc="mm")

                        def fn_evh(scalar, u=u, par=par, t=t):
                            return scalar.copy(out=ln2t16[:, 0, 2 * u + par, 128 * t:128 * (t + 1)],
                                               in_=ps_d0[:, 0:128])
                        ev_hi_d0 = em("scalar", fn_evh, waits=[("mm", tr_last_d0)], inc="ac")

                        def fn_evp(vector, u=u, par=par, t=t):
                            return vector.tensor_tensor(
                                out=ln2t16[:, 1, 2 * u + par, 128 * t:128 * (t + 1)],
                                in0=ps_d0[:, 0:128],
                                in1=ln2t16[:, 0, 2 * u + par, 128 * t:128 * (t + 1)],
                                op=ALU.subtract)
                        ev_d0 = em("vector", fn_evp, waits=[("mm", tr_last_d0), ("ac", ev_hi_d0)], inc="ve")
            if "ln2t" in D:
                em("sync", lambda s: s.dma_start(out=D["ln2t"],
                                                 in_=big0.ap().rearrange("p (a t) -> p a t", a=16)[:].rearrange("p a t -> p (a t)")),
                   waits=[("ve", bb.c["ve"]), ("ac", bb.c["ac"])], inc="st", incby=16)
            if upto == "D0":
                em("gpsimd", lambda g: g.nop(), waits=[("st", bb.c["st"])])
            bb.ph.realize(blk)
        psD0_cm[0].__exit__(None, None, None)
        for cm in reversed(d0_cms):
            cm.__exit__(None, None, None)
        for cm in reversed(early_cms):
            cm.__exit__(None, None, None)
        if upto == "D0":
            return nc, D

        d1_cms = []
        def _sd1(name, shape, dt=F32):
            cm = nc.sbuf_tensor(name, shape, dt)
            d1_cms.append(cm)
            return cm.__enter__()
        gWh_sb = _sd1("gateWh", [128, 2, 16, WN], F16)
        gWl_sb = _sd1("gateWl", [128, 2, 16, WN], F16)
        uWh_sb = _sd1("upWh", [128, 2, 16, WN], F16)
        uWl_sb = _sd1("upWl", [128, 2, 16, WN], F16)
        w32_sb = _sd1("w32", [128, 16, WN])
        gby_sb = _sd1("gby", [128, 4 * WN], U16)
        uby_sb = gby_sb
        gsc_sb = _sd1("gsc", [128, WN])
        usc_sb = gsc_sb
        bit4_sb = _sd1("bitp", [128, 2048], U16)
        t1_sb = _sd1("dect1", [128, 2, 1024], U16)
        t2_sb = _sd1("dect2", [128, 2, 1024], U16)
        dby_sb = _sd1("dby", [128, H // 2], U16)
        dWh_sb = _sd1("dWh", [128, 2, H], F16)
        dWl_sb = _sd1("dWl", [128, 2, H], F16)
        dsc_sb = _sd1("dsc", [128, H])
        inth_sb = _sd1("interh", [128, 2, T], F16)
        intl_sb = _sd1("interl", [128, 2, T], F16)
        gel_sb = _sd1("gel", [128, 2, 512])
        i32_sb = gel_sb
        psD1_cm = [nc.psum_tensor("psg", [128, 2, 512], F32),
                   nc.psum_tensor("psu", [128, 2, 512], F32),
                   nc.psum_tensor("psd", [128, H], F32)]
        ps_g, ps_u, ps_d = [cm.__enter__() for cm in psD1_cm]

        DEC_PLANES = ((0x0007, 6, 0, 0x0008, 12), (0x0070, 2, 0, 0x0080, 8),
                      (0x0700, 2, 1, 0x0800, 4), (0x7000, 6, 1, 0x8000, 0))

        def emit_nib_group(vector, by, n, grp):
            for si, s in grp:
                m1, sh1, dirn, m2, sh2 = DEC_PLANES[s]
                shop = ALU.logical_shift_left if dirn == 0 else ALU.logical_shift_right
                vector.tensor_scalar(out=t1_sb[:, si % 2, 0:n], in0=by[:, 0:n], scalar1=m1,
                                     scalar2=sh1, op0=ALU.bitwise_and, op1=shop)
                if sh2:
                    vector.tensor_scalar(out=t2_sb[:, si % 2, 0:n], in0=by[:, 0:n], scalar1=m2,
                                         scalar2=sh2, op0=ALU.bitwise_and, op1=ALU.logical_shift_left)
                else:
                    vector.tensor_scalar(out=t2_sb[:, si % 2, 0:n], in0=by[:, 0:n], scalar1=m2,
                                         scalar2=None, op0=ALU.bitwise_and)
            vector.drain()
            for si, s in grp:
                vector.tensor_scalar(out=bit4_sb[:, si * n:(si + 1) * n], in0=t1_sb[:, si % 2, 0:n],
                                     scalar1=0x40, scalar2=0xFA, op0=ALU.min, op1=ALU.mult)
                vector.tensor_tensor(out=t2_sb[:, si % 2, 0:n], in0=t1_sb[:, si % 2, 0:n],
                                     in1=t2_sb[:, si % 2, 0:n], op=ALU.add)
            vector.drain()
            for si, s in grp:
                vector.scalar_tensor_tensor(out=t1_sb[:, si % 2, 0:n], in0=t1_sb[:, si % 2, 0:n],
                                            scalar=0x80, in1=bit4_sb[:, si * n:(si + 1) * n],
                                            op0=ALU.min, op1=ALU.add)
            vector.drain()
            ins = None
            for si, s in grp:
                ins = vector.tensor_tensor(out=bit4_sb[:, si * n:(si + 1) * n], in0=t1_sb[:, si % 2, 0:n],
                                           in1=t2_sb[:, si % 2, 0:n], op=ALU.add)
            return ins

        def emit_gu_scales(vector, sc, hi_sb=None, wp=0):
            n = 4 * WN
            ins = None
            for s in range(4):
                par, half = s % 2, s // 2
                wout = bass.AP(w32_sb, par * WN + half * (WN // 2),
                               [[16 * WN, 128], [2 * WN, 8], [1, WN // 2]])
                scin = bass.AP(sc, half * (WN // 2), [[WN, 128], [0, 8], [1, WN // 2]])
                ins = vector.tensor_tensor(out=wout, in0=bit4_sb[:, s * n:(s + 1) * n].bitcast(mybir.dt.bfloat16),
                                           in1=scin, op=ALU.mult)
            if hi_sb is not None:
                vector.drain()
                for s in range(4):
                    par, half = s % 2, s // 2
                    hout = bass.AP(hi_sb, wp * 16 * WN + par * WN + half * (WN // 2),
                                   [[32 * WN, 128], [2 * WN, 8], [1, WN // 2]])
                    scin = bass.AP(sc, half * (WN // 2), [[WN, 128], [0, 8], [1, WN // 2]])
                    ins = vector.tensor_tensor(out=hout, in0=bit4_sb[:, s * n:(s + 1) * n].bitcast(mybir.dt.bfloat16),
                                               in1=scin, op=ALU.mult)
            return ins

        def emit_down_scales(vector, cl=None):
            n = H // 2
            ins = None
            for half in range(2):
                ins = vector.tensor_tensor(out=bass.AP(w32_sb, half * n, [[16 * WN, 128], [1, n]]),
                                           in0=bit4_sb[:, half * n:(half + 1) * n].bitcast(mybir.dt.bfloat16),
                                           in1=dsc_sb[:, half * n:(half + 1) * n], op=ALU.mult)
            if cl is not None:
                vector.drain()
                for half in range(2):
                    ins = vector.tensor_tensor(out=bass.AP(dWh_sb, cl * H + half * n, [[2 * H, 128], [1, n]]),
                                               in0=bit4_sb[:, half * n:(half + 1) * n].bitcast(mybir.dt.bfloat16),
                                               in1=dsc_sb[:, half * n:(half + 1) * n], op=ALU.mult)
            return ins

        w32_flat = bass.AP(w32_sb, 0, [[16 * WN, 128], [1, 16 * WN]])

        with nc.Block("phD1") as blk:
            bb.ph = Ph()
            v_acc0 = em("vector", lambda v: v.memset(acc_sb[:], 0.0), inc="ve")
            gw_used = [0, 0]
            uw_used = [0, 0]
            w32_free = 0
            gby_used = 0
            uby_used = 0
            dby_used = 0
            dsc_used = 0
            psg_free = [0, 0]
            psu_free = [0, 0]
            int_hi_done = 0
            int_free = [0, 0]
            gel_free = [0, 0]
            down_done = 0
            ps_d_free = v_acc0
            ST = {"gw_used": [0, 0], "uw_used": [0, 0], "w32_free": 0, "uby_used": 0,
                  "dsc_used": 0, "down_done": 0,
                  "ld_g": 0, "ld_u": 0, "ld_d": 0, "v_dg": 0, "v_dg_by": 0,
                  "a_gh": 0, "a_uh": 0, "a_dh": 0, "v_gl": {}, "v_ul": {}, "v_dd": {}}

            def frags_for(Bk):
                wp = Bk % 2
                fr = []

                def f_ldg(Bk=Bk):
                    def fn(sync):
                        sync.dma_start(out=gby_sb[:], in_=hgb[Bk]).then_inc(bb.h["ldg"], 16)
                        sync.dma_start(out=gsc_sb[0:64, :], in_=hgs[Bk]).then_inc(bb.h["ldg"], 16)
                        return sync.dma_start(out=gsc_sb[64:128, :], in_=hgs[Bk])
                    bb.c["ldg"] += 32
                    ST["ld_g"] = em("sync", fn, waits=[("ve", ST["uby_used"])], inc="ldg", incby=16)
                fr.append(f_ldg)

                def f_dg_a():
                    em("vector", lambda v: emit_nib_group(v, gby_sb, 4 * WN, ((0, 0), (1, 1))),
                       waits=[("ldg", ST["ld_g"])], inc="ve")
                fr.append(f_dg_a)

                def f_dg_b():
                    ST["v_dg_by"] = em("vector",
                                       lambda v: emit_nib_group(v, gby_sb, 4 * WN, ((2, 2), (3, 3))),
                                       inc="ve")
                fr.append(f_dg_b)

                def f_dg_s(wp=wp):
                    ST["v_dg"] = em("vector", lambda v, wp=wp: emit_gu_scales(v, gsc_sb, gWh_sb, wp),
                                    waits=[("gp", ST["w32_free"]), ("mm", ST["gw_used"][wp])],
                                    inc="ve")
                fr.append(f_dg_s)

                def f_gsplit(Bk=Bk, wp=wp):
                    ST["v_gl"][Bk] = em("gpsimd",
                                        lambda g, wp=wp: g.tensor_tensor(
                                            out=bass.AP(gWl_sb, wp * 16 * WN, [[32 * WN, 128], [1, 16 * WN]]),
                                            in0=w32_flat, in1=bass.AP(gWh_sb, wp * 16 * WN, [[32 * WN, 128], [1, 16 * WN]]),
                                            op=ALU.subtract),
                                        waits=[("ve", ST["v_dg"]), ("mm", ST["gw_used"][wp])], inc="gp")
                    ST["w32_free"] = ST["v_gl"][Bk]
                fr.append(f_gsplit)

                def f_ldu(Bk=Bk):
                    def fn(sync):
                        sync.dma_start(out=uby_sb[:], in_=hub[Bk]).then_inc(bb.h["ldg"], 16)
                        sync.dma_start(out=usc_sb[0:64, :], in_=hus[Bk]).then_inc(bb.h["ldg"], 16)
                        return sync.dma_start(out=usc_sb[64:128, :], in_=hus[Bk])
                    bb.c["ldg"] += 32
                    ST["ld_u"] = em("sync", fn, waits=[("ve", ST["v_dg"])], inc="ldg", incby=16)
                fr.append(f_ldu)

                def f_du_a():
                    em("vector", lambda v: emit_nib_group(v, uby_sb, 4 * WN, ((0, 0), (1, 1))),
                       waits=[("ldg", ST["ld_u"])], inc="ve")
                fr.append(f_du_a)

                def f_du_b():
                    em("vector", lambda v: emit_nib_group(v, uby_sb, 4 * WN, ((2, 2), (3, 3))),
                       inc="ve")
                fr.append(f_du_b)

                def f_du_s(wp=wp):
                    ST["v_du"] = em("vector", lambda v, wp=wp: emit_gu_scales(v, usc_sb, uWh_sb, wp),
                                    waits=[("gp", ST["w32_free"]), ("mm", ST["uw_used"][wp])],
                                    inc="ve")
                    ST["uby_used"] = ST["v_du"]
                fr.append(f_du_s)

                def f_usplit(Bk=Bk, wp=wp):
                    ST["v_ul"][Bk] = em("gpsimd",
                                        lambda g, wp=wp: g.tensor_tensor(
                                            out=bass.AP(uWl_sb, wp * 16 * WN, [[32 * WN, 128], [1, 16 * WN]]),
                                            in0=w32_flat, in1=bass.AP(uWh_sb, wp * 16 * WN, [[32 * WN, 128], [1, 16 * WN]]),
                                            op=ALU.subtract),
                                        waits=[("ve", ST["v_du"]), ("mm", ST["uw_used"][wp])], inc="gp")
                    ST["w32_free"] = ST["v_ul"][Bk]
                fr.append(f_usplit)

                if Bk % 2 == 1:
                    def f_ldd(Bk=Bk):
                        def fn(sync):
                            sync.dma_start(out=dby_sb[:], in_=hdb[Bk // 2]).then_inc(bb.h["ldd"], 16)
                            return sync.dma_start(out=dsc_sb[:], in_=hds[(Bk // 2) % 4])
                        bb.c["ldd"] += 16
                        ST["ld_d"] = em("sync", fn,
                                        waits=[("ve", ST["dsc_used"])],
                                        inc="ldd", incby=16)
                    fr.append(f_ldd)

                    def f_dd_a(cl):
                        em("vector", lambda v, cl=cl: emit_nib_group(v, dby_sb, H // 2,
                                                                     ((0, cl), (1, cl + 2))),
                           waits=[("ldd", ST["ld_d"])], inc="ve")
                    fr.append(lambda: f_dd_a(0))

                    def f_dd_s(cl):
                        ST["v_dds"] = em("vector", lambda v, cl=cl: emit_down_scales(v, cl),
                                         waits=[("gp", ST["w32_free"]), ("mm", ST["down_done"])],
                                         inc="ve")
                        if cl == 1:
                            ST["dsc_used"] = ST["v_dds"]
                    fr.append(lambda: f_dd_s(0))

                    def f_dsplit(cl, Bk=Bk):
                        ST["v_dd"][Bk * 2 + cl] = em("gpsimd", lambda g, cl=cl: g.tensor_tensor(
                            out=bass.AP(dWl_sb, cl * H, [[2 * H, 128], [1, H]]),
                            in0=bass.AP(w32_sb, 0, [[16 * WN, 128], [1, H]]),
                            in1=bass.AP(dWh_sb, cl * H, [[2 * H, 128], [1, H]]),
                            op=ALU.subtract),
                            waits=[("ve", ST["v_dds"]), ("mm", ST["down_done"])], inc="gp")
                        ST["w32_free"] = ST["v_dd"][Bk * 2 + cl]
                    fr.append(lambda: f_dsplit(0))
                    fr.append(lambda: f_dd_a(1))
                    fr.append(lambda: f_dd_s(1))
                    fr.append(lambda: f_dsplit(1))
                return fr

            pend = []

            def pump(n):
                for _ in range(n):
                    if pend:
                        pend.pop(0)()

            pend += frags_for(0)
            pump(99)
            int_ready = 0
            for Bk in range(d1_blocks):
                wp = Bk % 2
                islot = Bk % 2
                if Bk + 1 < d1_blocks:
                    pend += frags_for(Bk + 1)

                int_th = [0, 0]
                mgu = [0, 0]
                for th in range(2):
                    def fn_mgu(tensor, wp=wp, th=th):
                        ins = None
                        for Wh, Wl, ps, upi in ((gWh_sb, gWl_sb, ps_g, wp), (uWh_sb, uWl_sb, ps_u, wp)):
                            for kt in range(16):
                                tensor.ldweights(Wh[:, upi, kt, :])
                                for xp in (0, 1):
                                    ins = mm_nl(tensor, ps[:, th, :], Wh[:, upi, kt, :],
                                                ln2t16[:, xp, kt, 512 * th:512 * (th + 1)],
                                                start=(kt == 0 and xp == 0), stop=False)
                                tensor.ldweights(Wl[:, upi, kt, :])
                                ins = mm_nl(tensor, ps[:, th, :], Wl[:, upi, kt, :],
                                            ln2t16[:, 0, kt, 512 * th:512 * (th + 1)],
                                            start=False, stop=(kt == 15))
                        return ins
                    mgu[th] = em("tensor", fn_mgu,
                                 waits=[("gp", ST["v_gl"][Bk]), ("gp", ST["v_ul"][Bk]),
                                        ("ac", psg_free[th]), ("ve", psu_free[th])], inc="mm")
                ST["gw_used"][wp] = mgu[1]
                ST["uw_used"][wp] = mgu[1]
                pump(99)

                for th in range(2):
                    def fn_gel(scalar, th=th):
                        return scalar.activation(out=gel_sb[:, th, :], in_=ps_g[:, th, :],
                                                 func=AF.Gelu_apprx_tanh)
                    a_gel = em("scalar", fn_gel, waits=[("mm", mgu[th]), ("gp", gel_free[th])], inc="ac")
                    psg_free[th] = a_gel

                    def fn_i32(vector, th=th):
                        return vector.tensor_tensor(out=i32_sb[:, th, :], in0=gel_sb[:, th, :],
                                                    in1=ps_u[:, th, :], op=ALU.mult)
                    v_i32 = em("vector", fn_i32,
                               waits=[("ac", a_gel), ("mm", mgu[th]), ("ac", int_hi_done)], inc="ve")
                    psu_free[th] = v_i32

                    def fn_ih(scalar, th=th, islot=islot):
                        return scalar.copy(out=inth_sb[:, islot, 512 * th:512 * (th + 1)],
                                           in_=i32_sb[:, th, :])
                    int_hi_done = em("scalar", fn_ih,
                                     waits=[("ve", v_i32), ("mm", int_free[islot])], inc="ac")

                    def fn_il(gp, th=th, islot=islot):
                        return gp.tensor_tensor(out=intl_sb[:, islot, 512 * th:512 * (th + 1)],
                                                in0=i32_sb[:, th, :],
                                                in1=inth_sb[:, islot, 512 * th:512 * (th + 1)],
                                                op=ALU.subtract)
                    int_ready = em("gpsimd", fn_il, waits=[("ac", int_hi_done)], inc="gp")
                    gel_free[th] = int_ready
                    int_th[th] = int_ready
                ST["int_prev"] = ST.get("int_cur", [0, 0])
                ST["int_cur"] = list(int_th)
                if Bk % 2 == 0:
                    continue

                for t in range(NT):
                    def fn_md0(tensor, t=t):
                        ins = None
                        cl = 0
                        tensor.ldweights(inth_sb[:, cl, 128 * t:128 * (t + 1)])
                        for wop in (dWh_sb, dWl_sb):
                            for hc in range(4):
                                ins = mm_nl(tensor, ps_d[:, 512 * hc:512 * (hc + 1)],
                                            inth_sb[:, cl, 128 * t:128 * (t + 1)],
                                            wop[:, cl, 512 * hc:512 * (hc + 1)],
                                            start=(wop is dWh_sb), stop=False)
                        tensor.ldweights(intl_sb[:, cl, 128 * t:128 * (t + 1)])
                        for hc in range(4):
                            ins = mm_nl(tensor, ps_d[:, 512 * hc:512 * (hc + 1)],
                                        intl_sb[:, cl, 128 * t:128 * (t + 1)],
                                        dWh_sb[:, cl, 512 * hc:512 * (hc + 1)],
                                        start=False, stop=False)
                        return ins
                    md0 = em("tensor", fn_md0,
                             waits=[("gp", ST["v_dd"][Bk * 2 + 0]),
                                    ("gp", ST["int_prev"][t // 4]),
                                    ("ve", ps_d_free)], inc="mm")

                    def fn_md1(tensor, t=t):
                        ins = None
                        cl = 1
                        tensor.ldweights(inth_sb[:, cl, 128 * t:128 * (t + 1)])
                        for wop in (dWh_sb, dWl_sb):
                            for hc in range(4):
                                ins = mm_nl(tensor, ps_d[:, 512 * hc:512 * (hc + 1)],
                                            inth_sb[:, cl, 128 * t:128 * (t + 1)],
                                            wop[:, cl, 512 * hc:512 * (hc + 1)],
                                            start=False, stop=False)
                        tensor.ldweights(intl_sb[:, cl, 128 * t:128 * (t + 1)])
                        for hc in range(4):
                            ins = mm_nl(tensor, ps_d[:, 512 * hc:512 * (hc + 1)],
                                        intl_sb[:, cl, 128 * t:128 * (t + 1)],
                                        dWh_sb[:, cl, 512 * hc:512 * (hc + 1)],
                                        start=False, stop=True)
                        return ins
                    md = em("tensor", fn_md1,
                            waits=[("gp", ST["v_dd"][Bk * 2 + 1]),
                                   ("gp", ST["int_cur"][t // 4])], inc="mm")
                    ST["down_done"] = md
                    if t == NT - 1:
                        int_free = [md, md]

                    def fn_acc(vector, t=t, Bk=Bk):
                        if Bk == 1:
                            return vector.tensor_copy(out=acc_sb[:, t, :], in_=ps_d[:])
                        return vector.tensor_tensor(out=acc_sb[:, t, :], in0=acc_sb[:, t, :],
                                                    in1=ps_d[:], op=ALU.add)
                    ps_d_free = em("vector", fn_acc, waits=[("mm", md)], inc="ve")
            pump(99)
            if d1_blocks < NBLK:
                if "acc" in D:
                    def fn_dacc1(sync):
                        return sync.dma_start(out=D["acc"].rearrange("(a p) h -> p a h", p=128), in_=acc_sb[:])
                    em("sync", fn_dacc1, waits=[("ve", bb.c["ve"])], inc="st", incby=16)
                em("gpsimd", lambda g: g.nop(), waits=[("st", bb.c["st"])])
            bb.ph.realize(blk)
        for cm in reversed(psD1_cm):
            cm.__exit__(None, None, None)
        for cm in reversed(d1_cms):
            cm.__exit__(None, None, None)
        if d1_blocks < NBLK:
            return nc, D

        e_cms = []
        def _se(name, shape, dt=F32):
            cm = nc.sbuf_tensor(name, shape, dt)
            e_cms.append(cm)
            return cm.__enter__()
        x2e_sb = _se("x2e", [128, 2, H])
        ye_sb = _se("ye", [128, 2, H])
        with nc.Block("phE") as blk:
            bb.ph = Ph()
            x2slot = [0, 0]
            ystore = [0, 0]
            for t in range(NT):
                def fn_lde(sync, t=t):
                    return sync.dma_start(out=x2e_sb[:, t % 2, :],
                                          in_=hx2.rearrange("(a p) h -> p a h", p=128)[:, t, :])
                ld_e = em("sync", fn_lde,
                          waits=[("st0", st_after_c[0]), ("st1", st_after_c[1]),
                                 ("ve", x2slot[t % 2])],
                          inc="xd%d" % (t % 2), incby=16)

                def fn_y(vector, t=t):
                    x2ap = bass.AP(x2e_sb, (t % 2) * H, [[2 * H, 128], [1, 2], [2, 1024]])
                    yap = bass.AP(ye_sb, (t % 2) * H, [[2 * H, 128], [1, 2], [2, 1024]])
                    return vector.tensor_tensor(out=yap, in0=acc_sb[:, t, :], in1=x2ap, op=ALU.add)
                v_y = em("vector", fn_y, waits=[("xd%d" % (t % 2), ld_e),
                                                ("st%d" % (t % 2), ystore[t % 2])], inc="ve")
                x2slot[t % 2] = v_y

                def fn_sty(sync, t=t):
                    return sync.dma_start(out=hy.rearrange("(a p) h -> p a h", p=128)[:, t, :],
                                          in_=ye_sb[:, t % 2, :])
                ystore[t % 2] = em("sync", fn_sty, waits=[("ve", v_y)],
                                   inc="st%d" % (t % 2), incby=16)
            if "acc" in D:
                def fn_dacc(sync):
                    return sync.dma_start(out=D["acc"].rearrange("(a p) h -> p a h", p=128), in_=acc_sb[:])
                em("sync", fn_dacc, waits=[("ve", bb.c["ve"])], inc="st", incby=16)
            em("gpsimd", lambda g: g.nop(),
               waits=[("st", bb.c["st"]), ("st0", bb.c["st0"]), ("st1", bb.c["st1"])])
            bb.ph.realize(blk)
        for cm in reversed(e_cms):
            cm.__exit__(None, None, None)
        return nc, D


_NC_CACHE = {}


def kernel(**inputs):
    in_maps = _host_prep(inputs)
    if "main" not in _NC_CACHE:
        _NC_CACHE["main"] = build_nc(dbg=(), upto="E")
    nc, _ = _NC_CACHE["main"]
    res = run_bass_kernel_spmd(nc, in_maps, core_ids=list(range(8)))
    out = np.empty((4, 2 * T, H), np.float32)
    for core in range(8):
        b, half = divmod(core, 2)
        out[b, T * half:T * (half + 1)] = np.asarray(res.results[core]["y"])
    return out
